# version 02
# baseline (speedup 1.0000x reference)
import contextlib
from dataclasses import dataclass

import numpy as np
import concourse.bass as bass
import concourse.mybir as mybir

F32 = mybir.dt.float32
BF16 = mybir.dt.bfloat16
I32 = mybir.dt.int32
AF = mybir.ActivationFunctionType
OP = mybir.AluOpType
EPS_LN = 1e-5
EPS_RMS = 1e-6
NCORES = 8

ENG_ATTR = {"v": "vector", "s": "scalar", "t": "tensor", "g": "gpsimd", "y": "sync"}


def ceil_div(a, b):
    return (a + b - 1) // b


@dataclass
class Cfg:
    B: int = 2
    T: int = 8192
    H: int = 4096
    D: int = 8192
    R: int = 256
    N: int = 16
    K: int = 4
    G: int = 32
    GS: int = 128
    HALO: int = 128
    TC: int = 512
    SEG: int = 1024
    NG: int = 4

    @property
    def S(self):
        return self.B * self.T // NCORES

    @property
    def DS(self):
        return self.D // (NCORES // self.B)


class Buf:
    __slots__ = ("name", "w", "r", "sem")

    def __init__(self, name=""):
        self.name = name
        self.w = None
        self.r = []
        self.sem = None


class KB:

    def __init__(self, nc, es, engines=("v", "s", "t", "y")):
        self.nc = nc
        self.es = es
        self.engines = engines
        self.sem = {e: es.enter_context(nc.semaphore("sem_" + e)) for e in engines}
        self.cnt = {e: 0 for e in engines}
        self.plan = {e: [] for e in engines}
        self.known = {e: {} for e in engines}
        self.nsem = 0
        self.fence_w = {}

    def _buf_sem(self, b):
        if b.sem is None:
            b.sem = "bs%d_%s" % (self.nsem, b.name)
            self.nsem += 1
            self.sem[b.sem] = self.es.enter_context(
                self.nc.semaphore(b.sem))
            self.cnt[b.sem] = 0
        return b.sem

    def fence(self):
        for s, cc in self.cnt.items():
            if cc > 0:
                self.fence_w[s] = cc

    def op(self, eng, fn, reads=(), writes=(), inc=1, dma_buf=None):
        waits = dict(self.fence_w)
        for b in reads:
            if b.w:
                waits[b.w[0]] = max(waits.get(b.w[0], 0), b.w[1])
        for b in writes:
            if b.w:
                waits[b.w[0]] = max(waits.get(b.w[0], 0), b.w[1])
            for t in b.r:
                waits[t[0]] = max(waits.get(t[0], 0), t[1])
        kn = self.known[eng]
        wl = []
        for e2, cc in sorted(waits.items()):
            if cc > kn.get(e2, 0):
                wl.append((e2, cc))
                kn[e2] = cc
        if dma_buf is not None:
            sname = self._buf_sem(dma_buf)
        else:
            sname = eng
        self.cnt[sname] += inc
        tok = (sname, self.cnt[sname])
        self.plan[eng].append((wl, fn, inc, sname))
        for b in writes:
            b.w = tok
            b.r = []
        for b in reads:
            b.r.append(tok)
        return tok

    def dma(self, out, in_, reads=(), writes=(), ch="y", sem_buf=None):
        if sem_buf is None:
            sem_buf = writes[0] if writes else reads[0]
        return self.op(ch, lambda e: e.dma_start(out=out, in_=in_),
                       reads, writes, inc=16, dma_buf=sem_buf)

    def build_and_finish(self, block):
        sem = self.sem
        for e in self.engines:
            plan = self.plan[e]

            def body(eng, plan=plan, e=e):
                for wl, fn, inc, sname in plan:
                    for (we, wc) in wl:
                        eng.wait_ge(sem[we], wc)
                    fn(eng).then_inc(sem[sname], inc)
                if e == "y":
                    for s2, cc in self.cnt.items():
                        if cc > 0:
                            eng.wait_ge(sem[s2], cc)

            getattr(block, ENG_ATTR[e])(body)


def build_L1(c: Cfg):
    H, D, R, N, K = c.H, c.D, c.R, c.N, c.K
    S, HB, TC = c.S, c.HALO, c.TC
    SE = S + HB
    G, GS = c.G, c.GS
    O, I = H, H
    nH, nD, nO, nI = H // 128, D // 128, H // 128, H // 128
    RC = R + 2 * N
    rtiles = [(i * 128, min(128, R - i * 128)) for i in range(ceil_div(R, 128))]
    nR = len(rtiles)
    ogs = [(i * 128, min(128, RC - i * 128)) for i in range(ceil_div(RC, 128))]
    chunks = [(0, HB, True)] + [(HB + i * TC, TC, False) for i in range(S // TC)]

    nc = bass.Bass("TRN2", target_bir_lowering=False, num_devices=8)
    dt = nc.dram_tensor
    x = dt("x", [SE, H], F32, kind="ExternalInput").ap()
    w_in_t = dt("w_in_t", [2 * D // 128, 128, nH, 128], F32, kind="ExternalInput").ap()
    conv_w = dt("conv_w", [128, nD * K], F32, kind="ExternalInput").ap()
    conv_b = dt("conv_b", [128, nD], F32, kind="ExternalInput").ap()
    wxT = dt("wxT", [D, RC], F32, kind="ExternalInput").ap()
    ln_w_cat = dt("ln_w_cat", [1, RC], F32, kind="ExternalInput").ap()
    wdtT = dt("wdtT", [R, D], F32, kind="ExternalInput").ap()
    dt_bias = dt("dt_bias", [128, nD], F32, kind="ExternalInput").ap()
    qw = dt("qw", [O, I], I32, kind="ExternalInput").ap()
    qscale = dt("qscale", [128, nO * G], F32, kind="ExternalInput").ap()
    qzero = dt("qzero", [128, nO * G], F32, kind="ExternalInput").ap()
    fbias = dt("fbias", [128, nO], F32, kind="ExternalInput").ap()
    identb = dt("identb", [128, 128], BF16, kind="ExternalInput").ap()
    identf = dt("identf", [128, 128], F32, kind="ExternalInput").ap()

    dout = dt("dout", [nD, 128, S], F32, kind="ExternalOutput").ap()
    uout = dt("uout", [nD, 128, S], F32, kind="ExternalOutput").ap()
    gout = dt("gout", [nD, 128, S], F32, kind="ExternalOutput").ap()
    aout = dt("aout", [nO, 128, S], F32, kind="ExternalOutput").ap()
    bnT = dt("bnT", [N, S], F32, kind="ExternalOutput").ap()
    cnT = dt("cnT", [N, S], F32, kind="ExternalOutput").ap()

    w_in_bf = dt("w_in_bf", [2 * D // 128, 128, nH * 128], BF16, kind="Internal").ap()
    wdt_bf = dt("wdt_bf", [nR, 128, D], BF16, kind="Internal").ap()
    wT_scr = dt("wT_scr", [nO, 128, nI, 128], BF16, kind="Internal").ap()
    u_bf_scr = dt("u_bf_scr", [nD, 128, S], BF16, kind="Internal").ap()
    b_winbf = [Buf(f"winbf{i}") for i in range(2 * D // 128)]
    b_wdtbf = Buf("wdtbf")
    b_wtscr = [Buf(f"wtscr{i}") for i in range(nO)]
    b_uscr = [Buf(f"uscr{i}") for i in range(nD)]

    es = contextlib.ExitStack()
    with es:
        Sb = lambda nm, shp, dtp=F32: es.enter_context(nc.sbuf_tensor(nm, shp, dtp))
        kb = KB(nc, es)

        s_convw = Sb("s_convw", [128, nD * K]); b_convw = Buf()
        s_convb = Sb("s_convb", [128, nD]); b_convb = Buf()
        s_dtbias = Sb("s_dtbias", [128, nD]); b_dtbias = Buf()
        s_fbias = Sb("s_fbias", [128, nO]); b_fbias = Buf()
        s_lnw = Sb("s_lnw", [128, RC]); b_lnw = Buf()
        s_wxT = Sb("s_wxT", [128, nD * RC], BF16); b_wxT = Buf()
        s_qsc = Sb("s_qsc", [128, nO * G]); b_qsc = Buf()
        s_qzr = Sb("s_qzr", [128, nO * G]); b_qzr = Buf()
        s_idb = Sb("s_idb", [128, 128], BF16); b_idb = Buf()
        s_idf = Sb("s_idf", [128, 128], F32); b_idf = Buf()
        s_tail = Sb("s_tail", [128, nD * (K - 1)])
        b_tail = [Buf() for _ in range(nD)]
        s_cst = Sb("s_cst", [128, 8]); b_cst = Buf()
        kb.op("v", lambda e: e.memset(s_tail[:], 0.0),
              writes=b_tail)
        CST = {}
        for ci, cv in enumerate([EPS_LN, EPS_RMS, -0.5, -1.0, 1.0 / R, 1.0 / N]):
            kb.op("v", lambda e, ci=ci, cv=cv: e.memset(s_cst[:, ci:ci + 1], cv),
                  writes=[b_cst])
            CST[cv] = s_cst[:, ci:ci + 1]

        kb.dma(s_convw[:], conv_w, writes=[b_convw])
        kb.dma(s_convb[:], conv_b, writes=[b_convb])
        kb.dma(s_dtbias[:], dt_bias, writes=[b_dtbias])
        kb.dma(s_fbias[:], fbias, writes=[b_fbias])
        kb.dma(s_lnw[:], ln_w_cat.partition_broadcast(128), writes=[b_lnw])
        kb.dma(s_qsc[:], qscale, writes=[b_qsc])
        kb.dma(s_qzr[:], qzero, writes=[b_qzr])
        kb.dma(s_idb[:], identb, writes=[b_idb])
        kb.dma(s_idf[:], identf, writes=[b_idf])

        with contextlib.ExitStack() as es0:
            S0 = lambda nm, shp, dtp=F32: es0.enter_context(
                nc.sbuf_tensor(nm, shp, dtp))
            s_w32 = [S0(f"s_w32_{i}", [128, nH * 128]) for i in range(2)]
            b_w32 = [Buf() for _ in range(2)]
            s_wc = [S0(f"s_wc_{i}", [128, nH * 128], BF16) for i in range(2)]
            b_wc = [Buf() for _ in range(2)]
            s_q = [S0(f"s_q_{i}", [128, I], I32) for i in range(2)]
            b_q = [Buf() for _ in range(2)]
            s_dq = [S0(f"s_dq_{i}", [128, I], BF16) for i in range(2)]
            b_dq = [Buf() for _ in range(2)]
            s_tp = S0("s_tp", [128, 512], BF16)
            b_tp = Buf()
            ps0 = [es0.enter_context(nc.psum_tensor(f"ps0_{i}", [128, 128], BF16))
                   for i in range(2)]
            b_ps0 = [Buf() for _ in range(2)]

            for db in range(2 * D // 128):
                i = db % 2
                kb.dma(s_w32[i][:], w_in_t[db].rearrange("p hb oc -> p (hb oc)"),
                       writes=[b_w32[i]])
                kb.op("s", lambda e, i=i: e.copy(out=s_wc[i][:], in_=s_w32[i][:]),
                      reads=[b_w32[i]], writes=[b_wc[i]])
                kb.dma(w_in_bf[db], s_wc[i][:], reads=[b_wc[i]],
                       writes=[b_winbf[db]], sem_buf=b_wc[i], ch="s")
            for db in range(nD):
                i = db % 2
                kb.dma(s_w32[i][:, 0:RC], wxT[db * 128:(db + 1) * 128, :],
                       writes=[b_w32[i]])
                kb.op("s", lambda e, i=i, db=db: e.copy(
                    out=s_wxT[:, db * RC:(db + 1) * RC], in_=s_w32[i][:, 0:RC]),
                    reads=[b_w32[i]], writes=[b_wxT])
            nwj = ceil_div(D, nH * 128)
            for ri, (r0, rw) in enumerate(rtiles):
                for j in range(nwj):
                    w = min(nH * 128, D - j * nH * 128)
                    i = (ri * nwj + j) % 2
                    kb.dma(s_w32[i][:rw, 0:w],
                           wdtT[r0:r0 + rw, j * nH * 128:j * nH * 128 + w],
                           writes=[b_w32[i]])
                    kb.op("s", lambda e, i=i, rw=rw, w=w: e.copy(
                        out=s_wc[i][:rw, 0:w], in_=s_w32[i][:rw, 0:w]),
                        reads=[b_w32[i]], writes=[b_wc[i]])
                    kb.dma(wdt_bf[ri, :rw, j * nH * 128:j * nH * 128 + w],
                           s_wc[i][:rw, 0:w], reads=[b_wc[i]],
                           writes=[b_wdtbf], sem_buf=b_wc[i], ch="s")
            for ob in range(nO):
                i = ob % 2
                kb.dma(s_q[i][:], qw[ob * 128:(ob + 1) * 128, :], writes=[b_q[i]])
                for g in range(G):
                    kb.op("v", lambda e, i=i, ob=ob, g=g: e.tensor_scalar(
                        out=s_dq[i][:, g * GS:(g + 1) * GS],
                        in0=s_q[i][:, g * GS:(g + 1) * GS],
                        scalar1=s_qsc[:, ob * G + g:ob * G + g + 1],
                        scalar2=s_qzr[:, ob * G + g:ob * G + g + 1],
                        op0=OP.mult, op1=OP.add),
                        reads=[b_q[i], b_qsc, b_qzr], writes=[b_dq[i]])
                for ib in range(nI):
                    p = ib % 2
                    kb.op("t", lambda e, i=i, ib=ib, p=p: e.transpose(
                        ps0[p][:], s_dq[i][:, ib * 128:(ib + 1) * 128], s_idb[:]),
                        reads=[b_dq[i], b_idb], writes=[b_ps0[p]])
                    kb.op("s", lambda e, ib=ib, p=p: e.copy(
                        out=s_tp[:, (ib % 4) * 128:(ib % 4) * 128 + 128],
                        in_=ps0[p][:]),
                        reads=[b_ps0[p]], writes=[b_tp])
                    if ib % 4 == 3 or ib == nI - 1:
                        lo = (ib // 4) * 4
                        cnt = ib - lo + 1
                        kb.dma(wT_scr[ob, :, lo:lo + cnt, :]
                               .rearrange("p ib o -> p (ib o)"),
                               s_tp[:, 0:cnt * 128], reads=[b_tp],
                               writes=[b_wtscr[ob]], sem_buf=b_tp, ch="s")

        kb.fence()

        s_x = Sb("s_x", [128, H]); b_x = Buf()
        s_junk = Sb("s_junk", [128, H]); b_junk = Buf()
        s_col = Sb("s_col", [128, 8]); b_col = Buf()
        ntt_max = max(HB, TC) // 128
        s_xnb = Sb("s_xnb", [128, ntt_max * H], BF16)
        b_xnb = [Buf(f"xnb{i}") for i in range(ntt_max)]
        s_xnT = Sb("s_xnT", [128, nH * TC], BF16); b_xnT = Buf()
        s_wst = [Sb(f"s_wst_{i}", [128, max(nH, nI, nR) * 128], BF16)
                 for i in range(2)]
        b_wst = [Buf() for _ in range(2)]
        s_uraw = [Sb(f"s_uraw_{i}", [128, TC + K - 1]) for i in range(2)]
        b_uraw = [Buf() for _ in range(2)]
        s_cacc = [Sb(f"s_cacc_{i}", [128, TC]) for i in range(4)]
        b_cacc = [Buf(f"cacc{i}") for i in range(4)]
        s_uf = [Sb(f"s_uf_{i}", [128, TC]) for i in range(2)]
        b_uf = [Buf(f"uf{i}") for i in range(2)]
        s_ub = [Sb(f"s_ub_{i}", [128, TC], BF16) for i in range(2)]
        b_ub = [Buf(f"ub{i}") for i in range(2)]
        s_gt = [Sb(f"s_gt_{i}", [128, TC]) for i in range(2)]
        b_gt = [Buf(f"gt{i}") for i in range(2)]
        s_ac = [Sb(f"s_ac_{i}", [128, TC]) for i in range(2)]
        b_ac = [Buf(f"ac{i}") for i in range(2)]
        s_ubf = [Sb(f"s_ubf_{i}", [128, TC], BF16) for i in range(2)]
        b_ubf = [Buf() for _ in range(2)]
        s_sp = Sb("s_sp", [128, len(ogs) * TC]); b_sp = Buf()
        s_spT = Sb("s_spT", [128, RC]); b_spT = Buf()
        s_rtmp = Sb("s_rtmp", [128, R]); b_rtmp = Buf()
        s_dtn = Sb("s_dtn", [128, R], BF16); b_dtn = Buf()
        s_bcn = Sb("s_bcn", [128, 2 * N]); b_bcn = Buf()
        s_bcT = Sb("s_bcT", [128, 128]); b_bcT = Buf()
        s_dtnT = Sb("s_dtnT", [128, nR * TC], BF16); b_dtnT = Buf()
        s_sf = [Sb(f"s_sf_{i}", [128, TC]) for i in range(2)]
        b_sf = [Buf() for _ in range(2)]
        s_dl = [Sb(f"s_dl_{i}", [128, TC]) for i in range(2)]
        b_dl = [Buf(f"dl{i}") for i in range(2)]

        ps = [es.enter_context(nc.psum_tensor(f"ps_{i}", [128, 512], F32))
              for i in range(4)]
        b_ps = [Buf() for _ in range(4)]
        psTx = [es.enter_context(nc.psum_tensor(f"psTx{i}", [128, 512], BF16))
                for i in range(2)]
        b_psTx = [Buf(f"psTx{i}") for i in range(2)]
        psTf = es.enter_context(nc.psum_tensor("psTf", [128, 512], F32))
        b_psTf = Buf()

        colv = lambda j: s_col[:, j:j + 1]

        for (c0, w, halo) in chunks:
            c0r = c0 - HB
            ntt = w // 128
            for tt in range(ntt):
                t0 = c0 + tt * 128
                kb.dma(s_x[:], x[t0:t0 + 128, :], writes=[b_x])
                kb.op("s", lambda e: e.activation(
                    out=s_junk[:], in_=s_x[:], func=AF.Copy, accum_out=colv(0)),
                    reads=[b_x], writes=[b_junk, b_col])
                kb.op("s", lambda e: e.activation(
                    out=s_junk[:], in_=s_x[:], func=AF.Square, accum_out=colv(1)),
                    reads=[b_x], writes=[b_junk, b_col])
                kb.op("v", lambda e: e.tensor_scalar_mul(
                    out=colv(2), in0=colv(0), scalar1=1.0 / H),
                    reads=[b_col], writes=[b_col])
                kb.op("v", lambda e: e.tensor_scalar_mul(
                    out=colv(3), in0=colv(1), scalar1=1.0 / H),
                    reads=[b_col], writes=[b_col])
                kb.op("v", lambda e: e.tensor_mul(
                    out=colv(4), in0=colv(2), in1=colv(2)),
                    reads=[b_col], writes=[b_col])
                kb.op("v", lambda e: e.tensor_sub(
                    out=colv(3), in0=colv(3), in1=colv(4)),
                    reads=[b_col], writes=[b_col])
                kb.op("s", lambda e: e.activation(
                    out=colv(4), in_=colv(3), func=AF.Ln, bias=CST[EPS_LN]),
                    reads=[b_col, b_cst], writes=[b_col])
                kb.op("s", lambda e: e.activation(
                    out=colv(4), in_=colv(4), func=AF.Exp, scale=CST[-0.5]),
                    reads=[b_col, b_cst], writes=[b_col])
                kb.op("v", lambda e: e.scalar_tensor_tensor(
                    out=colv(5), in0=colv(2), scalar=-1.0, in1=colv(4),
                    op0=OP.mult, op1=OP.mult),
                    reads=[b_col], writes=[b_col])
                kb.op("v", lambda e, tt=tt: e.tensor_scalar(
                    out=s_xnb[:, tt * H:(tt + 1) * H], in0=s_x[:],
                    scalar1=colv(4), scalar2=colv(5),
                    op0=OP.mult, op1=OP.add),
                    reads=[b_x, b_col], writes=[b_xnb[tt]])
            for hb in range(nH):
                xi = hb % 2
                for tt in range(ntt):
                    kb.op("t", lambda e, hb=hb, tt=tt, xi=xi: e.transpose(
                        psTx[xi][:, tt * 128:(tt + 1) * 128],
                        s_xnb[:, tt * H + hb * 128:tt * H + hb * 128 + 128],
                        s_idb[:]),
                        reads=[b_xnb[tt], b_idb], writes=[b_psTx[xi]])
                kb.op("s", lambda e, hb=hb, xi=xi, w=w: e.copy(
                    out=s_xnT[:, hb * TC:hb * TC + w], in_=psTx[xi][:, 0:w]),
                    reads=[b_psTx[xi]], writes=[b_xnT])

            for db in range(nD):
                pi = db % 2
                kb.dma(s_wst[pi][:, 0:nH * 128], w_in_bf[db],
                       reads=[b_winbf[db]], writes=[b_wst[pi]])

                def fn_u(e, pi=pi, w=w):
                    for hb in range(nH):
                        inst = e.matmul(
                            ps[pi][:, 0:w],
                            s_wst[pi][:, hb * 128:(hb + 1) * 128],
                            s_xnT[:, hb * TC:hb * TC + w],
                            start=(hb == 0), stop=(hb == nH - 1))
                    return inst

                kb.op("t", fn_u, reads=[b_wst[pi], b_xnT], writes=[b_ps[pi]])
                ui = db % 2
                kb.op("s", lambda e, ui=ui, db=db: e.copy(
                    out=s_uraw[ui][:, 0:K - 1],
                    in_=s_tail[:, db * (K - 1):(db + 1) * (K - 1)]),
                    reads=[b_tail[db]], writes=[b_uraw[ui]])
                kb.op("s", lambda e, ui=ui, pi=pi, w=w: e.copy(
                    out=s_uraw[ui][:, K - 1:K - 1 + w], in_=ps[pi][:, 0:w]),
                    reads=[b_ps[pi]], writes=[b_uraw[ui]])
                kb.op("s", lambda e, ui=ui, db=db, w=w: e.copy(
                    out=s_tail[:, db * (K - 1):(db + 1) * (K - 1)],
                    in_=s_uraw[ui][:, w:w + K - 1]),
                    reads=[b_uraw[ui]], writes=[b_tail[db]])
                if halo:
                    continue
                cb = 2 * (db % 2)
                kb.op("v", lambda e, ui=ui, db=db, w=w, cb=cb: e.tensor_scalar_mul(
                    out=s_cacc[cb][:, 0:w], in0=s_uraw[ui][:, 0:w],
                    scalar1=s_convw[:, db * K:db * K + 1]),
                    reads=[b_uraw[ui], b_convw], writes=[b_cacc[cb]])
                for k in range(1, K):
                    kb.op("v", lambda e, ui=ui, db=db, k=k, w=w, cb=cb:
                          e.scalar_tensor_tensor(
                              out=s_cacc[cb + k % 2][:, 0:w],
                              in0=s_uraw[ui][:, k:k + w],
                              scalar=s_convw[:, db * K + k:db * K + k + 1],
                              in1=s_cacc[cb + (k - 1) % 2][:, 0:w],
                              op0=OP.mult, op1=OP.add),
                          reads=[b_uraw[ui], b_convw, b_cacc[cb + (k - 1) % 2]],
                          writes=[b_cacc[cb + k % 2]])
                fin = cb + (K - 1) % 2
                kb.op("s", lambda e, db=db, fin=fin, w=w, ui=ui: e.activation(
                    out=s_uf[ui][:, 0:w], in_=s_cacc[fin][:, 0:w], func=AF.Silu,
                    bias=s_convb[:, db:db + 1]),
                    reads=[b_cacc[fin], b_convb], writes=[b_uf[ui]])
                kb.op("s", lambda e, w=w, ui=ui: e.copy(
                    out=s_ub[ui][:, 0:w], in_=s_uf[ui][:, 0:w]),
                    reads=[b_uf[ui]], writes=[b_ub[ui]])
                kb.dma(uout[db, :, c0r:c0r + w], s_uf[ui][:, 0:w],
                       reads=[b_uf[ui]], ch="s")
                kb.dma(u_bf_scr[db, :, c0r:c0r + w], s_ub[ui][:, 0:w],
                       reads=[b_ub[ui]], writes=[b_uscr[db]], sem_buf=b_ub[ui],
                       ch="s")

            if halo:
                continue

            for db in range(nD):
                pi = db % 2
                kb.dma(s_wst[pi][:, 0:nH * 128], w_in_bf[nD + db],
                       reads=[b_winbf[nD + db]], writes=[b_wst[pi]])

                def fn_g(e, pi=pi, w=w):
                    for hb in range(nH):
                        inst = e.matmul(
                            ps[pi][:, 0:w],
                            s_wst[pi][:, hb * 128:(hb + 1) * 128],
                            s_xnT[:, hb * TC:hb * TC + w],
                            start=(hb == 0), stop=(hb == nH - 1))
                    return inst

                kb.op("t", fn_g, reads=[b_wst[pi], b_xnT], writes=[b_ps[pi]])
                kb.op("s", lambda e, pi=pi, w=w: e.activation(
                    out=s_gt[pi][:, 0:w], in_=ps[pi][:, 0:w], func=AF.Silu),
                    reads=[b_ps[pi]], writes=[b_gt[pi]])
                kb.dma(gout[db, :, c0r:c0r + w], s_gt[pi][:, 0:w],
                       reads=[b_gt[pi]], ch="s")

            for ob in range(nO):
                pi = ob % 2
                kb.dma(s_wst[pi][:, 0:nI * 128],
                       wT_scr[ob].rearrange("p ib o -> p (ib o)"),
                       reads=[b_wtscr[ob]], writes=[b_wst[pi]])

                def fn_f(e, pi=pi, w=w):
                    for ib in range(nI):
                        inst = e.matmul(
                            ps[pi][:, 0:w],
                            s_wst[pi][:, ib * 128:(ib + 1) * 128],
                            s_xnT[:, ib * TC:ib * TC + w],
                            start=(ib == 0), stop=(ib == nI - 1))
                    return inst

                kb.op("t", fn_f, reads=[b_wst[pi], b_xnT], writes=[b_ps[pi]])
                kb.op("v", lambda e, pi=pi, ob=ob, w=w: e.tensor_scalar_add(
                    out=s_ac[pi][:, 0:w], in0=ps[pi][:, 0:w],
                    scalar1=s_fbias[:, ob:ob + 1]),
                    reads=[b_ps[pi], b_fbias], writes=[b_ac[pi]])
                kb.dma(aout[ob, :, c0r:c0r + w], s_ac[pi][:, 0:w],
                       reads=[b_ac[pi]], ch="s")

            for db in range(nD):
                bi = db % 2
                kb.dma(s_ubf[bi][:, 0:w], u_bf_scr[db, :, c0r:c0r + w],
                       reads=[b_uscr[db]], writes=[b_ubf[bi]])
                for oi, (o0, ow) in enumerate(ogs):
                    kb.op("t", lambda e, oi=oi, o0=o0, ow=ow, bi=bi, db=db, w=w:
                          e.matmul(ps[oi][:ow, 0:w],
                                   s_wxT[:, db * RC + o0:db * RC + o0 + ow],
                                   s_ubf[bi][:, 0:w],
                                   start=(db == 0), stop=(db == nD - 1)),
                          reads=[b_wxT, b_ubf[bi]], writes=[b_ps[oi]])
            for oi, (o0, ow) in enumerate(ogs):
                kb.op("s", lambda e, oi=oi, ow=ow, w=w: e.copy(
                    out=s_sp[:ow, oi * TC:oi * TC + w], in_=ps[oi][:ow, 0:w]),
                    reads=[b_ps[oi]], writes=[b_sp])

            for tt in range(ntt):
                for oi, (o0, ow) in enumerate(ogs):
                    kb.op("t", lambda e, oi=oi, o0=o0, ow=ow, tt=tt: e.transpose(
                        psTf[:, o0:o0 + ow],
                        s_sp[:ow, oi * TC + tt * 128:oi * TC + tt * 128 + 128],
                        s_idf[:ow, 0:ow]),
                        reads=[b_sp, b_idf], writes=[b_psTf])
                kb.op("s", lambda e: e.copy(
                    out=s_spT[:, 0:RC], in_=psTf[:, 0:RC]),
                    reads=[b_psTf], writes=[b_spT])
                for (s0, sl, j) in [(0, R, 0), (R, N, 1), (R + N, N, 2)]:
                    kb.op("s", lambda e, s0=s0, sl=sl, j=j: e.activation(
                        out=s_junk[:, 0:sl], in_=s_spT[:, s0:s0 + sl],
                        func=AF.Square, accum_out=colv(j)),
                        reads=[b_spT], writes=[b_junk, b_col])
                    kb.op("s", lambda e, sl=sl, j=j: e.activation(
                        out=colv(j), in_=colv(j), func=AF.Ln,
                        scale=CST[1.0 / sl], bias=CST[EPS_RMS]),
                        reads=[b_col, b_cst], writes=[b_col])
                    kb.op("s", lambda e, j=j: e.activation(
                        out=colv(j), in_=colv(j), func=AF.Exp, scale=CST[-0.5]),
                        reads=[b_col, b_cst], writes=[b_col])
                kb.op("v", lambda e: e.tensor_scalar_mul(
                    out=s_rtmp[:, 0:R], in0=s_spT[:, 0:R], scalar1=colv(0)),
                    reads=[b_spT, b_col], writes=[b_rtmp])
                kb.op("v", lambda e: e.tensor_mul(
                    out=s_dtn[:, 0:R], in0=s_rtmp[:, 0:R], in1=s_lnw[:, 0:R]),
                    reads=[b_rtmp, b_lnw], writes=[b_dtn])
                kb.op("v", lambda e: e.tensor_scalar_mul(
                    out=s_bcn[:, 0:N], in0=s_spT[:, R:R + N], scalar1=colv(1)),
                    reads=[b_spT, b_col], writes=[b_bcn])
                kb.op("v", lambda e: e.tensor_scalar_mul(
                    out=s_bcn[:, N:2 * N], in0=s_spT[:, R + N:R + 2 * N],
                    scalar1=colv(2)),
                    reads=[b_spT, b_col], writes=[b_bcn])
                kb.op("v", lambda e: e.tensor_mul(
                    out=s_bcn[:, 0:N], in0=s_bcn[:, 0:N], in1=s_lnw[:, R:R + N]),
                    reads=[b_bcn, b_lnw], writes=[b_bcn])
                kb.op("v", lambda e: e.tensor_mul(
                    out=s_bcn[:, N:2 * N], in0=s_bcn[:, N:2 * N],
                    in1=s_lnw[:, R + N:R + 2 * N]),
                    reads=[b_bcn, b_lnw], writes=[b_bcn])
                kb.op("t", lambda e: e.transpose(
                    psTf[0:2 * N, 0:128], s_bcn[:, 0:2 * N], s_idf[:]),
                    reads=[b_bcn, b_idf], writes=[b_psTf])
                kb.op("s", lambda e: e.copy(
                    out=s_bcT[0:2 * N, 0:128], in_=psTf[0:2 * N, 0:128]),
                    reads=[b_psTf], writes=[b_bcT])
                kb.dma(bnT[:, c0r + tt * 128:c0r + tt * 128 + 128],
                       s_bcT[0:N, 0:128], reads=[b_bcT], ch="s")
                kb.dma(cnT[:, c0r + tt * 128:c0r + tt * 128 + 128],
                       s_bcT[N:2 * N, 0:128], reads=[b_bcT], ch="s")
                di = tt % 2
                for ri, (r0, rw) in enumerate(rtiles):
                    kb.op("t", lambda e, r0=r0, rw=rw, ri=ri, di=di: e.transpose(
                        psTx[di][0:rw, ri * 128:ri * 128 + 128],
                        s_dtn[:, r0:r0 + rw], s_idb[:]),
                        reads=[b_dtn, b_idb], writes=[b_psTx[di]])
                for ri, (r0, rw) in enumerate(rtiles):
                    kb.op("s", lambda e, ri=ri, tt=tt, rw=rw, di=di: e.copy(
                        out=s_dtnT[:rw, ri * TC + tt * 128:ri * TC + tt * 128 + 128],
                        in_=psTx[di][0:rw, ri * 128:ri * 128 + 128]),
                        reads=[b_psTx[di]], writes=[b_dtnT])

            for db in range(nD):
                pi = db % 2
                for ri, (r0, rw) in enumerate(rtiles):
                    kb.dma(s_wst[pi][:rw, ri * 128:ri * 128 + 128],
                           wdt_bf[ri, 0:rw, db * 128:(db + 1) * 128],
                           reads=[b_wdtbf], writes=[b_wst[pi]])

                def fn_d(e, pi=pi, w=w):
                    for ri, (r0, rw) in enumerate(rtiles):
                        inst = e.matmul(
                            ps[pi][:, 0:w],
                            s_wst[pi][:rw, ri * 128:ri * 128 + 128],
                            s_dtnT[:rw, ri * TC:ri * TC + w],
                            start=(ri == 0), stop=(ri == nR - 1))
                    return inst

                kb.op("t", fn_d, reads=[b_wst[pi], b_dtnT], writes=[b_ps[pi]])
                kb.op("s", lambda e, pi=pi, db=db, w=w: e.activation(
                    out=s_sf[0][:, 0:w], in_=ps[pi][:, 0:w], func=AF.Abs,
                    bias=s_dtbias[:, db:db + 1]),
                    reads=[b_ps[pi], b_dtbias], writes=[b_sf[0]])
                kb.op("s", lambda e, w=w: e.activation(
                    out=s_sf[1][:, 0:w], in_=s_sf[0][:, 0:w], func=AF.Exp,
                    scale=CST[-1.0]),
                    reads=[b_sf[0], b_cst], writes=[b_sf[1]])
                kb.op("s", lambda e, w=w: e.activation(
                    out=s_sf[0][:, 0:w], in_=s_sf[1][:, 0:w], func=AF.Ln,
                    bias=1.0),
                    reads=[b_sf[1]], writes=[b_sf[0]])
                kb.op("s", lambda e, pi=pi, db=db, w=w: e.activation(
                    out=s_sf[1][:, 0:w], in_=ps[pi][:, 0:w], func=AF.Relu,
                    bias=s_dtbias[:, db:db + 1]),
                    reads=[b_ps[pi], b_dtbias], writes=[b_sf[1]])
                kb.op("v", lambda e, w=w, pi=pi: e.tensor_add(
                    out=s_dl[pi][:, 0:w], in0=s_sf[0][:, 0:w],
                    in1=s_sf[1][:, 0:w]),
                    reads=[b_sf[0], b_sf[1]], writes=[b_dl[pi]])
                kb.dma(dout[db, :, c0r:c0r + w], s_dl[pi][:, 0:w],
                       reads=[b_dl[pi]], ch="s")

        block = es.enter_context(nc.Block())
        kb.build_and_finish(block)
    return nc


def build_L2(c: Cfg):
    T, DS, N, SEG = c.T, c.DS, c.N, c.SEG
    nb = DS // 128
    nseg = T // SEG

    nc = bass.Bass("TRN2", target_bir_lowering=False, num_devices=8)
    dt = nc.dram_tensor
    delta = dt("delta", [nb, 128, T], F32, kind="ExternalInput").ap()
    u = dt("u", [nb, 128, T], F32, kind="ExternalInput").ap()
    gate = dt("gate", [nb, 128, T], F32, kind="ExternalInput").ap()
    bnT = dt("bnT", [N, T], F32, kind="ExternalInput").ap()
    cnT = dt("cnT", [N, T], F32, kind="ExternalInput").ap()
    a_sl = dt("a_sl", [128, nb * N], F32, kind="ExternalInput").ap()
    dpar = dt("dpar", [128, nb], F32, kind="ExternalInput").ap()
    y = dt("y", [nb, 128, T], BF16, kind="ExternalOutput").ap()

    es = contextlib.ExitStack()
    with es:
        Sb = lambda nm, shp, dtp=F32: es.enter_context(nc.sbuf_tensor(nm, shp, dtp))
        kb = KB(nc, es, engines=("v", "s", "y"))

        s_A = Sb("s_A", [128, nb * N]); b_A = Buf("A")
        s_D = Sb("s_D", [128, nb]); b_D = Buf("D")
        s_hl = Sb("s_hl", [128, nb * N])
        b_hl = [Buf(f"hl{i}") for i in range(nb * N)]
        s_Bb = [Sb(f"s_Bb_{i}", [128, SEG]) for i in range(N)]
        b_Bb = [Buf(f"Bb{i}") for i in range(N)]
        s_Cb = [Sb(f"s_Cb_{i}", [128, SEG]) for i in range(N)]
        b_Cb = [Buf(f"Cb{i}") for i in range(N)]
        s_d = [Sb(f"s_d_{i}", [128, SEG]) for i in range(2)]
        b_d = [Buf(f"d{i}") for i in range(2)]
        s_u = [Sb(f"s_u_{i}", [128, SEG]) for i in range(2)]
        b_u = [Buf(f"u{i}") for i in range(2)]
        s_du = [Sb(f"s_du_{i}", [128, SEG]) for i in range(2)]
        b_du = [Buf(f"du{i}") for i in range(2)]
        s_g = [Sb(f"s_g_{i}", [128, SEG]) for i in range(2)]
        b_g = [Buf(f"g{i}") for i in range(2)]
        s_ya = [Sb(f"s_ya_{i}", [128, SEG]) for i in range(2)]
        b_ya = [Buf(f"ya{i}") for i in range(2)]
        s_dA = [Sb(f"s_dA_{i}", [128, SEG]) for i in range(2)]
        b_dA = [Buf(f"dA{i}") for i in range(2)]
        s_dB = Sb("s_dB", [128, SEG]); b_dB = Buf("dB")
        s_h = Sb("s_h", [128, SEG]); b_h = Buf("h")
        s_hc = Sb("s_hc", [128, SEG]); b_hc = Buf("hc")
        s_yb = [Sb(f"s_yb_{i}", [128, SEG], BF16) for i in range(2)]
        b_yb = [Buf(f"yb{i}") for i in range(2)]

        kb.dma(s_A[:], a_sl, writes=[b_A])
        kb.dma(s_D[:], dpar, writes=[b_D])

        for sg in range(nseg):
            sl = slice(sg * SEG, (sg + 1) * SEG)
            for n in range(N):
                kb.dma(s_Bb[n][:], bnT[n:n + 1, sl].partition_broadcast(128),
                       writes=[b_Bb[n]])
                kb.dma(s_Cb[n][:], cnT[n:n + 1, sl].partition_broadcast(128),
                       writes=[b_Cb[n]])
            for b in range(nb):
                i = b % 2
                kb.dma(s_d[i][:], delta[b, :, sl], writes=[b_d[i]])
                kb.dma(s_u[i][:], u[b, :, sl], writes=[b_u[i]])
                kb.dma(s_g[i][:], gate[b, :, sl], writes=[b_g[i]])
                kb.op("v", lambda e, i=i: e.tensor_mul(
                    out=s_du[i][:], in0=s_d[i][:], in1=s_u[i][:]),
                    reads=[b_d[i], b_u[i]], writes=[b_du[i]])
                for n in range(N):
                    hli = b * N + n
                    ai = n % 2
                    kb.op("s", lambda e, i=i, b=b, n=n, ai=ai: e.activation(
                        out=s_dA[ai][:], in_=s_d[i][:], func=AF.Exp,
                        scale=s_A[:, b * N + n:b * N + n + 1]),
                        reads=[b_d[i], b_A], writes=[b_dA[ai]])
                    kb.op("v", lambda e, i=i, n=n: e.tensor_mul(
                        out=s_dB[:], in0=s_du[i][:], in1=s_Bb[n][:]),
                        reads=[b_du[i], b_Bb[n]], writes=[b_dB])
                    if sg == 0:
                        kb.op("v", lambda e, ai=ai: e.tensor_tensor_scan(
                            out=s_h[:], data0=s_dA[ai][:], data1=s_dB[:],
                            initial=0.0, op0=OP.mult, op1=OP.add),
                            reads=[b_dA[ai], b_dB], writes=[b_h])
                    else:
                        kb.op("v", lambda e, ai=ai, hli=hli: e.tensor_tensor_scan(
                            out=s_h[:], data0=s_dA[ai][:], data1=s_dB[:],
                            initial=s_hl[:, hli:hli + 1],
                            op0=OP.mult, op1=OP.add),
                            reads=[b_dA[ai], b_dB, b_hl[hli]], writes=[b_h])
                    if sg < nseg - 1:
                        kb.op("s", lambda e, hli=hli: e.copy(
                            out=s_hl[:, hli:hli + 1], in_=s_h[:, SEG - 1:SEG]),
                            reads=[b_h], writes=[b_hl[hli]])
                    if n == 0:
                        kb.op("v", lambda e, i=i, n=n: e.tensor_mul(
                            out=s_ya[i][:], in0=s_h[:], in1=s_Cb[n][:]),
                            reads=[b_h, b_Cb[n]], writes=[b_ya[i]])
                    else:
                        kb.op("v", lambda e, n=n: e.tensor_mul(
                            out=s_hc[:], in0=s_h[:], in1=s_Cb[n][:]),
                            reads=[b_h, b_Cb[n]], writes=[b_hc])
                        kb.op("v", lambda e, i=i: e.tensor_add(
                            out=s_ya[i][:], in0=s_ya[i][:], in1=s_hc[:]),
                            reads=[b_hc, b_ya[i]], writes=[b_ya[i]])
                kb.op("v", lambda e, i=i, b=b: e.scalar_tensor_tensor(
                    out=s_ya[i][:], in0=s_u[i][:], scalar=s_D[:, b:b + 1],
                    in1=s_ya[i][:], op0=OP.mult, op1=OP.add),
                    reads=[b_u[i], b_D, b_ya[i]], writes=[b_ya[i]])
                kb.op("v", lambda e, i=i: e.tensor_mul(
                    out=s_yb[i][:], in0=s_ya[i][:], in1=s_g[i][:]),
                    reads=[b_ya[i], b_g[i]], writes=[b_yb[i]])
                kb.dma(y[b, :, sl], s_yb[i][:], reads=[b_yb[i]], ch="s")

        block = es.enter_context(nc.Block())
        kb.build_and_finish(block)
    return nc


def build_L3(c: Cfg):
    H, D = c.H, c.D
    S, TC = c.S, c.TC
    nH, nD = H // 128, D // 128
    chunks = [(i * TC, TC) for i in range(S // TC)]

    nc = bass.Bass("TRN2", target_bir_lowering=False, num_devices=8)
    dt = nc.dram_tensor
    w_out_t = dt("w_out_t", [nH, 128, nD, 128], F32, kind="ExternalInput").ap()
    yin = dt("yin", [nD, 128, S], BF16, kind="ExternalInput").ap()
    acc = dt("acc", [nH, 128, S], F32, kind="ExternalInput").ap()
    x = dt("x", [S, H], F32, kind="ExternalInput").ap()
    identf = dt("identf", [128, 128], F32, kind="ExternalInput").ap()
    out = dt("out", [S, H], F32, kind="ExternalOutput").ap()
    w_out_bf = dt("w_out_bf", [nH, 128, nD * 128], BF16, kind="Internal").ap()
    b_wobf = [Buf(f"wobf{i}") for i in range(nH)]

    es = contextlib.ExitStack()
    with es:
        Sb = lambda nm, shp, dtp=F32: es.enter_context(nc.sbuf_tensor(nm, shp, dtp))
        kb = KB(nc, es)

        s_idf = Sb("s_idf", [128, 128], F32); b_idf = Buf()
        kb.dma(s_idf[:], identf, writes=[b_idf])

        with contextlib.ExitStack() as es0:
            s_w32 = [es0.enter_context(
                nc.sbuf_tensor(f"s_w32_{i}", [128, nD * 128], F32))
                for i in range(2)]
            b_w32 = [Buf() for _ in range(2)]
            s_wcc = [es0.enter_context(
                nc.sbuf_tensor(f"s_wcc_{i}", [128, nD * 128], BF16))
                for i in range(2)]
            b_wcc = [Buf() for _ in range(2)]
            for hb in range(nH):
                i = hb % 2
                kb.dma(s_w32[i][:], w_out_t[hb].rearrange("p db oc -> p (db oc)"),
                       writes=[b_w32[i]])
                kb.op("s", lambda e, i=i: e.copy(out=s_wcc[i][:], in_=s_w32[i][:]),
                      reads=[b_w32[i]], writes=[b_wcc[i]])
                kb.dma(w_out_bf[hb], s_wcc[i][:], reads=[b_wcc[i]],
                       writes=[b_wobf[hb]], sem_buf=b_wcc[i], ch="s")

        kb.fence()
        s_yall = Sb("s_yall", [128, nD * TC], BF16); b_yall = Buf()
        s_wst = [Sb(f"s_wst_{i}", [128, nD * 128], BF16) for i in range(2)]
        b_wst = [Buf() for _ in range(2)]
        s_acc = Sb("s_acc", [128, TC]); b_acc = Buf()
        s_oh = Sb("s_oh", [128, TC]); b_oh = Buf()
        s_fin = [Sb(f"s_fin_{i}", [128, H]) for i in range(TC // 128)]
        b_fin = [Buf() for _ in range(TC // 128)]
        s_x = Sb("s_x", [128, H]); b_x = Buf()

        ps = [es.enter_context(nc.psum_tensor(f"ps_{i}", [128, 512], F32))
              for i in range(2)]
        b_ps = [Buf() for _ in range(2)]
        psT3 = [es.enter_context(nc.psum_tensor(f"psT3_{i}", [128, 128], F32))
                for i in range(2)]
        b_psT3 = [Buf(f"psT3{i}") for i in range(2)]

        for (c0, w) in chunks:
            for db in range(nD):
                kb.dma(s_yall[:, db * TC:db * TC + w], yin[db, :, c0:c0 + w],
                       writes=[b_yall])
            for hb in range(nH):
                pi = hb % 2
                kb.dma(s_wst[pi][:], w_out_bf[hb],
                       reads=[b_wobf[hb]], writes=[b_wst[pi]])

                def fn_o(e, pi=pi, w=w):
                    for db in range(nD):
                        inst = e.matmul(
                            ps[pi][:, 0:w],
                            s_wst[pi][:, db * 128:(db + 1) * 128],
                            s_yall[:, db * TC:db * TC + w],
                            start=(db == 0), stop=(db == nD - 1))
                    return inst

                kb.op("t", fn_o, reads=[b_wst[pi], b_yall], writes=[b_ps[pi]])
                kb.dma(s_acc[:, 0:w], acc[hb, :, c0:c0 + w], writes=[b_acc])
                kb.op("v", lambda e, pi=pi, w=w: e.tensor_add(
                    out=s_oh[:, 0:w], in0=ps[pi][:, 0:w], in1=s_acc[:, 0:w]),
                    reads=[b_ps[pi], b_acc], writes=[b_oh])
                for tt in range(w // 128):
                    zi = tt % 2
                    kb.op("t", lambda e, tt=tt, zi=zi: e.transpose(
                        psT3[zi][:], s_oh[:, tt * 128:(tt + 1) * 128], s_idf[:]),
                        reads=[b_oh, b_idf], writes=[b_psT3[zi]])
                    kb.op("s", lambda e, hb=hb, tt=tt, zi=zi: e.copy(
                        out=s_fin[tt][:, hb * 128:(hb + 1) * 128],
                        in_=psT3[zi][:]),
                        reads=[b_psT3[zi]], writes=[b_fin[tt]])
            for tt in range(w // 128):
                t0 = c0 + tt * 128
                kb.dma(s_x[:], x[t0:t0 + 128, :], writes=[b_x])
                kb.op("v", lambda e, tt=tt: e.tensor_add(
                    out=s_fin[tt][:], in0=s_fin[tt][:], in1=s_x[:]),
                    reads=[b_x], writes=[b_fin[tt]])
                kb.dma(out[t0:t0 + 128, :], s_fin[tt][:], reads=[b_fin[tt]],
                       ch="s")

        block = es.enter_context(nc.Block())
        kb.build_and_finish(block)
    return nc


def host_prepare(c: Cfg, inputs):
    H, D, R, N, K, G, GS = c.H, c.D, c.R, c.N, c.K, c.G, c.GS
    S, HB = c.S, c.HALO
    nH, nD, nO = H // 128, D // 128, H // 128

    xfull = np.asarray(inputs["L0_attn_hidden_states"], np.float32)
    B, T, _ = xfull.shape
    xf = xfull.reshape(B * T, H)
    W_in = np.asarray(inputs["L0_attn_W_in"], np.float32)
    w_in_t = np.ascontiguousarray(
        W_in.reshape(2 * D // 128, 128, nH, 128).transpose(0, 3, 2, 1))
    conv_w = np.ascontiguousarray(
        np.asarray(inputs["L0_attn_conv_w"], np.float32)
        .reshape(nD, 128, K).transpose(1, 0, 2).reshape(128, nD * K))
    conv_b = np.ascontiguousarray(
        np.asarray(inputs["L0_attn_conv_b"], np.float32).reshape(nD, 128).T)
    wxT = np.ascontiguousarray(np.asarray(inputs["L0_attn_Wx"], np.float32).T)
    ln_w_cat = np.concatenate([
        np.asarray(inputs["L0_attn_dt_ln_w"], np.float32),
        np.asarray(inputs["L0_attn_b_ln_w"], np.float32),
        np.asarray(inputs["L0_attn_c_ln_w"], np.float32)]).reshape(1, -1)
    wdtT = np.ascontiguousarray(np.asarray(inputs["L0_attn_Wdt"], np.float32).T)
    dt_bias = np.ascontiguousarray(
        np.asarray(inputs["L0_attn_dt_bias"], np.float32).reshape(nD, 128).T)
    qw = np.ascontiguousarray(
        np.asarray(inputs["L0_ffn_qweight"], np.int32).reshape(H, H))
    qscale = np.ascontiguousarray(
        np.asarray(inputs["L0_ffn_scales"], np.float32)
        .reshape(nO, 128, G).transpose(1, 0, 2).reshape(128, nO * G))
    qzero = np.ascontiguousarray(
        np.asarray(inputs["L0_ffn_scaled_zeros"], np.float32)
        .reshape(nO, 128, G).transpose(1, 0, 2).reshape(128, nO * G))
    fbias = np.ascontiguousarray(
        np.asarray(inputs["L0_ffn_bias"], np.float32).reshape(nO, 128).T)
    import ml_dtypes
    identb = np.eye(128, dtype=ml_dtypes.bfloat16)
    identf = np.eye(128, dtype=np.float32)

    in_maps1 = []
    for cc in range(NCORES):
        t0 = cc * S
        xe = np.zeros((HB + S, H), np.float32)
        xe[HB:] = xf[t0:t0 + S]
        bstart = (t0 // T) * T
        nh = min(HB, t0 - bstart)
        if nh > 0:
            xe[HB - nh:HB] = xf[t0 - nh:t0]
        in_maps1.append(dict(
            x=xe, w_in_t=w_in_t, conv_w=conv_w, conv_b=conv_b, wxT=wxT,
            ln_w_cat=ln_w_cat, wdtT=wdtT, dt_bias=dt_bias, qw=qw,
            qscale=qscale, qzero=qzero, fbias=fbias, identb=identb,
            identf=identf))
    return in_maps1


def host_l2_maps(c: Cfg, r1):
    H, D, N = c.H, c.D, c.N
    S = c.S
    nD = D // 128
    nbat = NCORES // c.B
    DS = c.DS
    nb = DS // 128

    def cat_t(key, nblk):
        outs = []
        for b in range(c.B):
            cores = [r1[b * nbat + i][key] for i in range(nbat)]
            outs.append(np.concatenate(cores, axis=2))
        return outs

    dlt = cat_t("dout", nD)
    uu = cat_t("uout", nD)
    gg = cat_t("gout", nD)
    bn = cat_t("bnT", None)
    cn = cat_t("cnT", None)
    A = np.asarray(None) if False else None
    in_maps2 = []
    for cc in range(NCORES):
        b = cc // (NCORES // c.B)
        q = cc % (NCORES // c.B)
        sl = slice(q * nb, (q + 1) * nb)
        in_maps2.append(dict(
            delta=np.ascontiguousarray(dlt[b][sl]),
            u=np.ascontiguousarray(uu[b][sl]),
            gate=np.ascontiguousarray(gg[b][sl]),
            bnT=np.ascontiguousarray(bn[b]),
            cnT=np.ascontiguousarray(cn[b]),
        ))
    return in_maps2


def run_all(c: Cfg, inputs, runner):
    import ml_dtypes
    H, D, N = c.H, c.D, c.N
    S = c.S
    nD, nH = D // 128, H // 128
    nbat = NCORES // c.B
    DS = c.DS
    nb = DS // 128

    in1 = host_prepare(c, inputs)
    nc1 = build_L1(c)
    r1 = runner(nc1, in1)

    A = np.asarray(inputs["L0_attn_A"], np.float32).reshape(nD, 128, N)
    Dp = np.asarray(inputs["L0_attn_D_param"], np.float32).reshape(nD, 128)
    in2 = []
    per_b = {}
    for b in range(c.B):
        cores = [r1[b * nbat + i] for i in range(nbat)]
        per_b[b] = dict(
            delta=np.concatenate([r["dout"] for r in cores], axis=2),
            u=np.concatenate([r["uout"] for r in cores], axis=2),
            gate=np.concatenate([r["gout"] for r in cores], axis=2),
            bnT=np.concatenate([r["bnT"] for r in cores], axis=1),
            cnT=np.concatenate([r["cnT"] for r in cores], axis=1),
        )
    for cc in range(NCORES):
        b, q = cc // nbat, cc % nbat
        sl = slice(q * nb, (q + 1) * nb)
        in2.append(dict(
            delta=np.ascontiguousarray(per_b[b]["delta"][sl]),
            u=np.ascontiguousarray(per_b[b]["u"][sl]),
            gate=np.ascontiguousarray(per_b[b]["gate"][sl]),
            bnT=per_b[b]["bnT"], cnT=per_b[b]["cnT"],
            a_sl=np.ascontiguousarray(
                A[sl].transpose(1, 0, 2).reshape(128, nb * N)),
            dpar=np.ascontiguousarray(Dp[sl].T)))
    nc2 = build_L2(c)
    r2 = runner(nc2, in2)

    W_out = np.asarray(inputs["L0_attn_W_out"], np.float32)
    w_out_t = np.ascontiguousarray(
        W_out.reshape(nH, 128, nD, 128).transpose(0, 3, 2, 1))
    identf = np.eye(128, dtype=np.float32)
    y_b = {}
    for b in range(c.B):
        y_b[b] = np.concatenate(
            [r2[b * nbat + q]["y"] for q in range(nbat)], axis=0)
    xfull = np.asarray(inputs["L0_attn_hidden_states"], np.float32)
    xf = xfull.reshape(c.B * c.T, H)
    in3 = []
    for cc in range(NCORES):
        t0 = cc * S
        b = t0 // c.T
        tb = t0 - b * c.T
        in3.append(dict(
            w_out_t=w_out_t,
            yin=np.ascontiguousarray(y_b[b][:, :, tb:tb + S]),
            acc=r1[cc]["aout"],
            x=np.ascontiguousarray(xf[t0:t0 + S]),
            identf=identf))
    nc3 = build_L3(c)
    r3 = runner(nc3, in3)

    out = np.concatenate([r3[cc]["out"] for cc in range(NCORES)], axis=0)
    return out.reshape(c.B, c.T, H)


from concourse.bass_utils import run_bass_kernel_spmd


def _hw_runner(nc, in_maps):
    res = run_bass_kernel_spmd(nc, in_maps, core_ids=list(range(NCORES)))
    return res.results


def kernel(**inputs):
    c = Cfg()
    out = run_all(c, inputs, _hw_runner)
    return np.ascontiguousarray(out.astype(np.float32))


# version 04 (final)
# speedup vs baseline: 1.0920x; 1.0920x over previous
import contextlib
from dataclasses import dataclass

import numpy as np
import concourse.bass as bass
import concourse.mybir as mybir

F32 = mybir.dt.float32
BF16 = mybir.dt.bfloat16
I32 = mybir.dt.int32
AF = mybir.ActivationFunctionType
OP = mybir.AluOpType
EPS_LN = 1e-5
EPS_RMS = 1e-6
NCORES = 8

ENG_ATTR = {"v": "vector", "s": "scalar", "t": "tensor", "g": "gpsimd", "y": "sync"}


def ceil_div(a, b):
    return (a + b - 1) // b


@dataclass
class Cfg:
    B: int = 2
    T: int = 8192
    H: int = 4096
    D: int = 8192
    R: int = 256
    N: int = 16
    K: int = 4
    G: int = 32
    GS: int = 128
    HALO: int = 128
    TC: int = 512
    SEG: int = 1024
    NG: int = 4

    @property
    def S(self):
        return self.B * self.T // NCORES

    @property
    def DS(self):
        return self.D // (NCORES // self.B)


class Buf:
    __slots__ = ("name", "w", "r", "sem")

    def __init__(self, name=""):
        self.name = name
        self.w = None
        self.r = []
        self.sem = None


class KB:

    def __init__(self, nc, es, engines=("v", "s", "t", "y")):
        self.nc = nc
        self.es = es
        self.engines = engines
        self.sem = {e: es.enter_context(nc.semaphore("sem_" + e)) for e in engines}
        self.cnt = {e: 0 for e in engines}
        self.plan = {e: [] for e in engines}
        self.known = {e: {} for e in engines}
        self.nsem = 0
        self.fence_w = {}

    def _buf_sem(self, b):
        if b.sem is None:
            b.sem = "bs%d_%s" % (self.nsem, b.name)
            self.nsem += 1
            self.sem[b.sem] = self.es.enter_context(
                self.nc.semaphore(b.sem))
            self.cnt[b.sem] = 0
        return b.sem

    def fence(self):
        for s, cc in self.cnt.items():
            if cc > 0:
                self.fence_w[s] = cc

    def op(self, eng, fn, reads=(), writes=(), inc=1, dma_buf=None):
        waits = dict(self.fence_w)
        for b in reads:
            if b.w:
                waits[b.w[0]] = max(waits.get(b.w[0], 0), b.w[1])
        for b in writes:
            if b.w:
                waits[b.w[0]] = max(waits.get(b.w[0], 0), b.w[1])
            for t in b.r:
                waits[t[0]] = max(waits.get(t[0], 0), t[1])
        kn = self.known[eng]
        wl = []
        for e2, cc in sorted(waits.items()):
            if cc > kn.get(e2, 0):
                wl.append((e2, cc))
                kn[e2] = cc
        if dma_buf is not None:
            sname = self._buf_sem(dma_buf)
        else:
            sname = eng
        self.cnt[sname] += inc
        tok = (sname, self.cnt[sname])
        self.plan[eng].append((wl, fn, inc, sname))
        for b in writes:
            b.w = tok
            b.r = []
        for b in reads:
            b.r.append(tok)
        return tok

    def dma(self, out, in_, reads=(), writes=(), ch="y", sem_buf=None):
        if sem_buf is None:
            sem_buf = writes[0] if writes else reads[0]
        return self.op(ch, lambda e: e.dma_start(out=out, in_=in_),
                       reads, writes, inc=16, dma_buf=sem_buf)

    def build_and_finish(self, block):
        sem = self.sem
        for e in self.engines:
            plan = self.plan[e]

            def body(eng, plan=plan, e=e):
                for wl, fn, inc, sname in plan:
                    for (we, wc) in wl:
                        eng.wait_ge(sem[we], wc)
                    fn(eng).then_inc(sem[sname], inc)
                if e == "y":
                    for s2, cc in self.cnt.items():
                        if cc > 0:
                            eng.wait_ge(sem[s2], cc)

            getattr(block, ENG_ATTR[e])(body)


def build_L1(c: Cfg):
    H, D, R, N, K = c.H, c.D, c.R, c.N, c.K
    S, HB, TC = c.S, c.HALO, c.TC
    SE = S + HB
    G, GS = c.G, c.GS
    O, I = H, H
    nH, nD, nO, nI = H // 128, D // 128, H // 128, H // 128
    RC = R + 2 * N
    rtiles = [(i * 128, min(128, R - i * 128)) for i in range(ceil_div(R, 128))]
    nR = len(rtiles)
    ogs = [(i * 128, min(128, RC - i * 128)) for i in range(ceil_div(RC, 128))]
    chunks = [(0, HB, True)] + [(HB + i * TC, TC, False) for i in range(S // TC)]

    nc = bass.Bass("TRN2", target_bir_lowering=False, num_devices=8)
    dt = nc.dram_tensor
    x = dt("x", [SE, H], F32, kind="ExternalInput").ap()
    w_in_t = dt("w_in_t", [2 * D // 128, 128, nH, 128], F32, kind="ExternalInput").ap()
    conv_w = dt("conv_w", [128, nD * K], F32, kind="ExternalInput").ap()
    conv_b = dt("conv_b", [128, nD], F32, kind="ExternalInput").ap()
    wxT = dt("wxT", [D, RC], F32, kind="ExternalInput").ap()
    ln_w_cat = dt("ln_w_cat", [1, RC], F32, kind="ExternalInput").ap()
    wdtT = dt("wdtT", [R, D], F32, kind="ExternalInput").ap()
    dt_bias = dt("dt_bias", [128, nD], F32, kind="ExternalInput").ap()
    qw = dt("qw", [O, I], I32, kind="ExternalInput").ap()
    qscale = dt("qscale", [128, nO * G], F32, kind="ExternalInput").ap()
    qzero = dt("qzero", [128, nO * G], F32, kind="ExternalInput").ap()
    fbias = dt("fbias", [128, nO], F32, kind="ExternalInput").ap()
    identb = dt("identb", [128, 128], BF16, kind="ExternalInput").ap()
    identf = dt("identf", [128, 128], F32, kind="ExternalInput").ap()

    dout = dt("dout", [nD, 128, S], F32, kind="ExternalOutput").ap()
    uout = dt("uout", [nD, 128, S], F32, kind="ExternalOutput").ap()
    gout = dt("gout", [nD, 128, S], F32, kind="ExternalOutput").ap()
    aout = dt("aout", [nO, 128, S], F32, kind="ExternalOutput").ap()
    bnT = dt("bnT", [N, S], F32, kind="ExternalOutput").ap()
    cnT = dt("cnT", [N, S], F32, kind="ExternalOutput").ap()

    w_in_bf = dt("w_in_bf", [2 * D // 128, 128, nH * 128], BF16, kind="Internal").ap()
    wdt_bf = dt("wdt_bf", [nR, 128, D], BF16, kind="Internal").ap()
    wx_bf = dt("wx_bf", [nD, 128, RC], BF16, kind="Internal").ap()
    b_wxbf = [Buf(f"wxbf{i}") for i in range(nD)]
    wT_scr = dt("wT_scr", [nO, 128, nI, 128], BF16, kind="Internal").ap()
    u_bf_scr = dt("u_bf_scr", [nD, 128, S], BF16, kind="Internal").ap()
    b_winbf = [Buf(f"winbf{i}") for i in range(2 * D // 128)]
    b_wdtbf = Buf("wdtbf")
    b_wtscr = [Buf(f"wtscr{i}") for i in range(nO)]
    b_uscr = [Buf(f"uscr{i}") for i in range(nD)]

    es = contextlib.ExitStack()
    with es:
        Sb = lambda nm, shp, dtp=F32: es.enter_context(nc.sbuf_tensor(nm, shp, dtp))
        kb = KB(nc, es)

        s_convw = Sb("s_convw", [128, nD * K]); b_convw = Buf()
        s_convb = Sb("s_convb", [128, nD]); b_convb = Buf()
        s_dtbias = Sb("s_dtbias", [128, nD]); b_dtbias = Buf()
        s_fbias = Sb("s_fbias", [128, nO]); b_fbias = Buf()
        s_lnw = Sb("s_lnw", [128, RC]); b_lnw = Buf()
        s_qsc = Sb("s_qsc", [128, nO * G]); b_qsc = Buf()
        s_qzr = Sb("s_qzr", [128, nO * G]); b_qzr = Buf()
        s_idb = Sb("s_idb", [128, 128], BF16); b_idb = Buf()
        s_idf = Sb("s_idf", [128, 128], F32); b_idf = Buf()
        s_tail = Sb("s_tail", [128, nD * (K - 1)])
        b_tail = [Buf() for _ in range(nD)]
        s_cst = Sb("s_cst", [128, 8]); b_cst = Buf()
        kb.op("v", lambda e: e.memset(s_tail[:], 0.0),
              writes=b_tail)
        CST = {}
        for ci, cv in enumerate([EPS_LN, EPS_RMS, -0.5, -1.0, 1.0 / R, 1.0 / N]):
            kb.op("v", lambda e, ci=ci, cv=cv: e.memset(s_cst[:, ci:ci + 1], cv),
                  writes=[b_cst])
            CST[cv] = s_cst[:, ci:ci + 1]

        kb.dma(s_convw[:], conv_w, writes=[b_convw])
        kb.dma(s_convb[:], conv_b, writes=[b_convb])
        kb.dma(s_dtbias[:], dt_bias, writes=[b_dtbias])
        kb.dma(s_fbias[:], fbias, writes=[b_fbias])
        kb.dma(s_lnw[:], ln_w_cat.partition_broadcast(128), writes=[b_lnw])
        kb.dma(s_qsc[:], qscale, writes=[b_qsc])
        kb.dma(s_qzr[:], qzero, writes=[b_qzr])
        kb.dma(s_idb[:], identb, writes=[b_idb])
        kb.dma(s_idf[:], identf, writes=[b_idf])

        with contextlib.ExitStack() as es0:
            S0 = lambda nm, shp, dtp=F32: es0.enter_context(
                nc.sbuf_tensor(nm, shp, dtp))
            s_w32 = [S0(f"s_w32_{i}", [128, nH * 128]) for i in range(2)]
            b_w32 = [Buf() for _ in range(2)]
            s_wc = [S0(f"s_wc_{i}", [128, nH * 128], BF16) for i in range(2)]
            b_wc = [Buf() for _ in range(2)]
            s_q = [S0(f"s_q_{i}", [128, I], I32) for i in range(2)]
            b_q = [Buf() for _ in range(2)]
            s_dq = [S0(f"s_dq_{i}", [128, I], BF16) for i in range(2)]
            b_dq = [Buf() for _ in range(2)]
            s_tp = S0("s_tp", [128, 512], BF16)
            b_tp = Buf()
            ps0 = [es0.enter_context(nc.psum_tensor(f"ps0_{i}", [128, 128], BF16))
                   for i in range(2)]
            b_ps0 = [Buf() for _ in range(2)]

            for db in range(2 * D // 128):
                i = db % 2
                kb.dma(s_w32[i][:], w_in_t[db].rearrange("p hb oc -> p (hb oc)"),
                       writes=[b_w32[i]])
                kb.op("s", lambda e, i=i: e.copy(out=s_wc[i][:], in_=s_w32[i][:]),
                      reads=[b_w32[i]], writes=[b_wc[i]])
                kb.dma(w_in_bf[db], s_wc[i][:], reads=[b_wc[i]],
                       writes=[b_winbf[db]], sem_buf=b_wc[i], ch="s")
            for db in range(nD):
                i = db % 2
                kb.dma(s_w32[i][:, 0:RC], wxT[db * 128:(db + 1) * 128, :],
                       writes=[b_w32[i]])
                kb.op("s", lambda e, i=i: e.copy(
                    out=s_wc[i][:, 0:RC], in_=s_w32[i][:, 0:RC]),
                    reads=[b_w32[i]], writes=[b_wc[i]])
                kb.dma(wx_bf[db], s_wc[i][:, 0:RC], reads=[b_wc[i]],
                       writes=[b_wxbf[db]], sem_buf=b_wc[i], ch="s")
            nwj = ceil_div(D, nH * 128)
            for ri, (r0, rw) in enumerate(rtiles):
                for j in range(nwj):
                    w = min(nH * 128, D - j * nH * 128)
                    i = (ri * nwj + j) % 2
                    kb.dma(s_w32[i][:rw, 0:w],
                           wdtT[r0:r0 + rw, j * nH * 128:j * nH * 128 + w],
                           writes=[b_w32[i]])
                    kb.op("s", lambda e, i=i, rw=rw, w=w: e.copy(
                        out=s_wc[i][:rw, 0:w], in_=s_w32[i][:rw, 0:w]),
                        reads=[b_w32[i]], writes=[b_wc[i]])
                    kb.dma(wdt_bf[ri, :rw, j * nH * 128:j * nH * 128 + w],
                           s_wc[i][:rw, 0:w], reads=[b_wc[i]],
                           writes=[b_wdtbf], sem_buf=b_wc[i], ch="s")
            for ob in range(nO):
                i = ob % 2
                kb.dma(s_q[i][:], qw[ob * 128:(ob + 1) * 128, :], writes=[b_q[i]])
                for g in range(G):
                    kb.op("v", lambda e, i=i, ob=ob, g=g: e.tensor_scalar(
                        out=s_dq[i][:, g * GS:(g + 1) * GS],
                        in0=s_q[i][:, g * GS:(g + 1) * GS],
                        scalar1=s_qsc[:, ob * G + g:ob * G + g + 1],
                        scalar2=s_qzr[:, ob * G + g:ob * G + g + 1],
                        op0=OP.mult, op1=OP.add),
                        reads=[b_q[i], b_qsc, b_qzr], writes=[b_dq[i]])
                for ib in range(nI):
                    p = ib % 2
                    kb.op("t", lambda e, i=i, ib=ib, p=p: e.transpose(
                        ps0[p][:], s_dq[i][:, ib * 128:(ib + 1) * 128], s_idb[:]),
                        reads=[b_dq[i], b_idb], writes=[b_ps0[p]])
                    kb.op("s", lambda e, ib=ib, p=p: e.copy(
                        out=s_tp[:, (ib % 4) * 128:(ib % 4) * 128 + 128],
                        in_=ps0[p][:]),
                        reads=[b_ps0[p]], writes=[b_tp])
                    if ib % 4 == 3 or ib == nI - 1:
                        lo = (ib // 4) * 4
                        cnt = ib - lo + 1
                        kb.dma(wT_scr[ob, :, lo:lo + cnt, :]
                               .rearrange("p ib o -> p (ib o)"),
                               s_tp[:, 0:cnt * 128], reads=[b_tp],
                               writes=[b_wtscr[ob]], sem_buf=b_tp, ch="s")

        kb.fence()

        s_x = Sb("s_x", [128, H]); b_x = Buf()
        s_junk = Sb("s_junk", [128, H]); b_junk = Buf()
        s_col = Sb("s_col", [128, 8]); b_col = Buf()
        ntt_max = max(HB, TC) // 128
        s_xnb = Sb("s_xnb", [128, ntt_max * H], BF16)
        b_xnb = [Buf(f"xnb{i}") for i in range(ntt_max)]
        s_xnT = Sb("s_xnT", [128, nH * TC], BF16); b_xnT = Buf()
        NW = 3
        s_wst = [Sb(f"s_wst_{i}", [128, max(nH, nI, nR) * 128], BF16)
                 for i in range(NW)]
        b_wst = [Buf(f"wst{i}") for i in range(NW)]
        s_uraw = [Sb(f"s_uraw_{i}", [128, TC + K - 1]) for i in range(2)]
        b_uraw = [Buf() for _ in range(2)]
        s_cacc = [Sb(f"s_cacc_{i}", [128, TC]) for i in range(4)]
        b_cacc = [Buf(f"cacc{i}") for i in range(4)]
        s_uf = [Sb(f"s_uf_{i}", [128, TC]) for i in range(2)]
        b_uf = [Buf(f"uf{i}") for i in range(2)]
        s_ub = [Sb(f"s_ub_{i}", [128, TC], BF16) for i in range(2)]
        b_ub = [Buf(f"ub{i}") for i in range(2)]
        s_gt = [Sb(f"s_gt_{i}", [128, TC]) for i in range(2)]
        b_gt = [Buf(f"gt{i}") for i in range(2)]
        s_ac = [Sb(f"s_ac_{i}", [128, TC]) for i in range(2)]
        b_ac = [Buf(f"ac{i}") for i in range(2)]
        s_ubf = [Sb(f"s_ubf_{i}", [128, TC], BF16) for i in range(2)]
        b_ubf = [Buf() for _ in range(2)]
        s_sp = Sb("s_sp", [128, len(ogs) * TC]); b_sp = Buf()
        s_spT = Sb("s_spT", [128, RC]); b_spT = Buf()
        s_rtmp = Sb("s_rtmp", [128, R]); b_rtmp = Buf()
        s_dtn = Sb("s_dtn", [128, R], BF16); b_dtn = Buf()
        s_bcn = Sb("s_bcn", [128, 2 * N]); b_bcn = Buf()
        s_bcT = Sb("s_bcT", [128, 128]); b_bcT = Buf()
        s_dtnT = Sb("s_dtnT", [128, nR * TC], BF16); b_dtnT = Buf()
        s_sf = [Sb(f"s_sf_{i}", [128, TC]) for i in range(2)]
        b_sf = [Buf() for _ in range(2)]
        s_dl = [Sb(f"s_dl_{i}", [128, TC]) for i in range(2)]
        b_dl = [Buf(f"dl{i}") for i in range(2)]

        ps = [es.enter_context(nc.psum_tensor(f"ps_{i}", [128, 512], F32))
              for i in range(4)]
        b_ps = [Buf() for _ in range(4)]
        psTx = [es.enter_context(nc.psum_tensor(f"psTx{i}", [128, 512], BF16))
                for i in range(2)]
        b_psTx = [Buf(f"psTx{i}") for i in range(2)]
        psTf = es.enter_context(nc.psum_tensor("psTf", [128, 512], F32))
        b_psTf = Buf()

        colv = lambda j: s_col[:, j:j + 1]

        for (c0, w, halo) in chunks:
            c0r = c0 - HB
            ntt = w // 128
            for tt in range(ntt):
                t0 = c0 + tt * 128
                kb.dma(s_x[:], x[t0:t0 + 128, :], writes=[b_x])
                kb.op("s", lambda e: e.activation(
                    out=s_junk[:], in_=s_x[:], func=AF.Copy, accum_out=colv(0)),
                    reads=[b_x], writes=[b_junk, b_col])
                kb.op("s", lambda e: e.activation(
                    out=s_junk[:], in_=s_x[:], func=AF.Square, accum_out=colv(1)),
                    reads=[b_x], writes=[b_junk, b_col])
                kb.op("v", lambda e: e.tensor_scalar_mul(
                    out=colv(2), in0=colv(0), scalar1=1.0 / H),
                    reads=[b_col], writes=[b_col])
                kb.op("v", lambda e: e.tensor_scalar_mul(
                    out=colv(3), in0=colv(1), scalar1=1.0 / H),
                    reads=[b_col], writes=[b_col])
                kb.op("v", lambda e: e.tensor_mul(
                    out=colv(4), in0=colv(2), in1=colv(2)),
                    reads=[b_col], writes=[b_col])
                kb.op("v", lambda e: e.tensor_sub(
                    out=colv(3), in0=colv(3), in1=colv(4)),
                    reads=[b_col], writes=[b_col])
                kb.op("s", lambda e: e.activation(
                    out=colv(4), in_=colv(3), func=AF.Ln, bias=CST[EPS_LN]),
                    reads=[b_col, b_cst], writes=[b_col])
                kb.op("s", lambda e: e.activation(
                    out=colv(4), in_=colv(4), func=AF.Exp, scale=CST[-0.5]),
                    reads=[b_col, b_cst], writes=[b_col])
                kb.op("v", lambda e: e.scalar_tensor_tensor(
                    out=colv(5), in0=colv(2), scalar=-1.0, in1=colv(4),
                    op0=OP.mult, op1=OP.mult),
                    reads=[b_col], writes=[b_col])
                kb.op("v", lambda e, tt=tt: e.tensor_scalar(
                    out=s_xnb[:, tt * H:(tt + 1) * H], in0=s_x[:],
                    scalar1=colv(4), scalar2=colv(5),
                    op0=OP.mult, op1=OP.add),
                    reads=[b_x, b_col], writes=[b_xnb[tt]])
            for hb in range(nH):
                xi = hb % 2
                for tt in range(ntt):
                    kb.op("t", lambda e, hb=hb, tt=tt, xi=xi: e.transpose(
                        psTx[xi][:, tt * 128:(tt + 1) * 128],
                        s_xnb[:, tt * H + hb * 128:tt * H + hb * 128 + 128],
                        s_idb[:]),
                        reads=[b_xnb[tt], b_idb], writes=[b_psTx[xi]])
                kb.op("s", lambda e, hb=hb, xi=xi, w=w: e.copy(
                    out=s_xnT[:, hb * TC:hb * TC + w], in_=psTx[xi][:, 0:w]),
                    reads=[b_psTx[xi]], writes=[b_xnT])

            for db in range(nD):
                pi = db % 2
                wi = db % NW
                kb.dma(s_wst[wi][:, 0:nH * 128], w_in_bf[db],
                       reads=[b_winbf[db]], writes=[b_wst[wi]])

                def fn_u(e, pi=pi, wi=wi, w=w):
                    for hb in range(nH):
                        inst = e.matmul(
                            ps[pi][:, 0:w],
                            s_wst[wi][:, hb * 128:(hb + 1) * 128],
                            s_xnT[:, hb * TC:hb * TC + w],
                            start=(hb == 0), stop=(hb == nH - 1))
                    return inst

                kb.op("t", fn_u, reads=[b_wst[wi], b_xnT], writes=[b_ps[pi]])
                ui = db % 2
                kb.op("s", lambda e, ui=ui, db=db: e.copy(
                    out=s_uraw[ui][:, 0:K - 1],
                    in_=s_tail[:, db * (K - 1):(db + 1) * (K - 1)]),
                    reads=[b_tail[db]], writes=[b_uraw[ui]])
                kb.op("s", lambda e, ui=ui, pi=pi, w=w: e.copy(
                    out=s_uraw[ui][:, K - 1:K - 1 + w], in_=ps[pi][:, 0:w]),
                    reads=[b_ps[pi]], writes=[b_uraw[ui]])
                kb.op("s", lambda e, ui=ui, db=db, w=w: e.copy(
                    out=s_tail[:, db * (K - 1):(db + 1) * (K - 1)],
                    in_=s_uraw[ui][:, w:w + K - 1]),
                    reads=[b_uraw[ui]], writes=[b_tail[db]])
                if halo:
                    continue
                cb = 2 * (db % 2)
                kb.op("v", lambda e, ui=ui, db=db, w=w, cb=cb: e.tensor_scalar_mul(
                    out=s_cacc[cb][:, 0:w], in0=s_uraw[ui][:, 0:w],
                    scalar1=s_convw[:, db * K:db * K + 1]),
                    reads=[b_uraw[ui], b_convw], writes=[b_cacc[cb]])
                for k in range(1, K):
                    kb.op("v", lambda e, ui=ui, db=db, k=k, w=w, cb=cb:
                          e.scalar_tensor_tensor(
                              out=s_cacc[cb + k % 2][:, 0:w],
                              in0=s_uraw[ui][:, k:k + w],
                              scalar=s_convw[:, db * K + k:db * K + k + 1],
                              in1=s_cacc[cb + (k - 1) % 2][:, 0:w],
                              op0=OP.mult, op1=OP.add),
                          reads=[b_uraw[ui], b_convw, b_cacc[cb + (k - 1) % 2]],
                          writes=[b_cacc[cb + k % 2]])
                fin = cb + (K - 1) % 2
                kb.op("s", lambda e, db=db, fin=fin, w=w, ui=ui: e.activation(
                    out=s_uf[ui][:, 0:w], in_=s_cacc[fin][:, 0:w], func=AF.Silu,
                    bias=s_convb[:, db:db + 1]),
                    reads=[b_cacc[fin], b_convb], writes=[b_uf[ui]])
                kb.op("s", lambda e, w=w, ui=ui: e.copy(
                    out=s_ub[ui][:, 0:w], in_=s_uf[ui][:, 0:w]),
                    reads=[b_uf[ui]], writes=[b_ub[ui]])
                kb.dma(uout[db, :, c0r:c0r + w], s_uf[ui][:, 0:w],
                       reads=[b_uf[ui]], ch="s")
                kb.dma(u_bf_scr[db, :, c0r:c0r + w], s_ub[ui][:, 0:w],
                       reads=[b_ub[ui]], writes=[b_uscr[db]], sem_buf=b_ub[ui],
                       ch="s")

            if halo:
                continue

            for db in range(nD):
                pi = db % 2
                wi = db % NW
                kb.dma(s_wst[wi][:, 0:nH * 128], w_in_bf[nD + db],
                       reads=[b_winbf[nD + db]], writes=[b_wst[wi]])

                def fn_g(e, pi=pi, wi=wi, w=w):
                    for hb in range(nH):
                        inst = e.matmul(
                            ps[pi][:, 0:w],
                            s_wst[wi][:, hb * 128:(hb + 1) * 128],
                            s_xnT[:, hb * TC:hb * TC + w],
                            start=(hb == 0), stop=(hb == nH - 1))
                    return inst

                kb.op("t", fn_g, reads=[b_wst[wi], b_xnT], writes=[b_ps[pi]])
                kb.op("s", lambda e, pi=pi, w=w: e.activation(
                    out=s_gt[pi][:, 0:w], in_=ps[pi][:, 0:w], func=AF.Silu),
                    reads=[b_ps[pi]], writes=[b_gt[pi]])
                kb.dma(gout[db, :, c0r:c0r + w], s_gt[pi][:, 0:w],
                       reads=[b_gt[pi]], ch="s")

            for ob in range(nO):
                pi = ob % 2
                wi = ob % NW
                kb.dma(s_wst[wi][:, 0:nI * 128],
                       wT_scr[ob].rearrange("p ib o -> p (ib o)"),
                       reads=[b_wtscr[ob]], writes=[b_wst[wi]])

                def fn_f(e, pi=pi, wi=wi, w=w):
                    for ib in range(nI):
                        inst = e.matmul(
                            ps[pi][:, 0:w],
                            s_wst[wi][:, ib * 128:(ib + 1) * 128],
                            s_xnT[:, ib * TC:ib * TC + w],
                            start=(ib == 0), stop=(ib == nI - 1))
                    return inst

                kb.op("t", fn_f, reads=[b_wst[wi], b_xnT], writes=[b_ps[pi]])
                kb.op("v", lambda e, pi=pi, ob=ob, w=w: e.tensor_scalar_add(
                    out=s_ac[pi][:, 0:w], in0=ps[pi][:, 0:w],
                    scalar1=s_fbias[:, ob:ob + 1]),
                    reads=[b_ps[pi], b_fbias], writes=[b_ac[pi]])
                kb.dma(aout[ob, :, c0r:c0r + w], s_ac[pi][:, 0:w],
                       reads=[b_ac[pi]], ch="s")

            for db in range(nD):
                bi = db % 2
                wi = db % NW
                kb.dma(s_ubf[bi][:, 0:w], u_bf_scr[db, :, c0r:c0r + w],
                       reads=[b_uscr[db]], writes=[b_ubf[bi]])
                kb.dma(s_wst[wi][:, 0:RC], wx_bf[db],
                       reads=[b_wxbf[db]], writes=[b_wst[wi]])
                for oi, (o0, ow) in enumerate(ogs):
                    kb.op("t", lambda e, oi=oi, o0=o0, ow=ow, bi=bi, wi=wi,
                          w=w, db=db:
                          e.matmul(ps[oi][:ow, 0:w],
                                   s_wst[wi][:, o0:o0 + ow],
                                   s_ubf[bi][:, 0:w],
                                   start=(db == 0), stop=(db == nD - 1)),
                          reads=[b_wst[wi], b_ubf[bi]], writes=[b_ps[oi]])
            for oi, (o0, ow) in enumerate(ogs):
                kb.op("s", lambda e, oi=oi, ow=ow, w=w: e.copy(
                    out=s_sp[:ow, oi * TC:oi * TC + w], in_=ps[oi][:ow, 0:w]),
                    reads=[b_ps[oi]], writes=[b_sp])

            for tt in range(ntt):
                for oi, (o0, ow) in enumerate(ogs):
                    kb.op("t", lambda e, oi=oi, o0=o0, ow=ow, tt=tt: e.transpose(
                        psTf[:, o0:o0 + ow],
                        s_sp[:ow, oi * TC + tt * 128:oi * TC + tt * 128 + 128],
                        s_idf[:ow, 0:ow]),
                        reads=[b_sp, b_idf], writes=[b_psTf])
                kb.op("s", lambda e: e.copy(
                    out=s_spT[:, 0:RC], in_=psTf[:, 0:RC]),
                    reads=[b_psTf], writes=[b_spT])
                for (s0, sl, j) in [(0, R, 0), (R, N, 1), (R + N, N, 2)]:
                    kb.op("s", lambda e, s0=s0, sl=sl, j=j: e.activation(
                        out=s_junk[:, 0:sl], in_=s_spT[:, s0:s0 + sl],
                        func=AF.Square, accum_out=colv(j)),
                        reads=[b_spT], writes=[b_junk, b_col])
                    kb.op("s", lambda e, sl=sl, j=j: e.activation(
                        out=colv(j), in_=colv(j), func=AF.Ln,
                        scale=CST[1.0 / sl], bias=CST[EPS_RMS]),
                        reads=[b_col, b_cst], writes=[b_col])
                    kb.op("s", lambda e, j=j: e.activation(
                        out=colv(j), in_=colv(j), func=AF.Exp, scale=CST[-0.5]),
                        reads=[b_col, b_cst], writes=[b_col])
                kb.op("v", lambda e: e.tensor_scalar_mul(
                    out=s_rtmp[:, 0:R], in0=s_spT[:, 0:R], scalar1=colv(0)),
                    reads=[b_spT, b_col], writes=[b_rtmp])
                kb.op("v", lambda e: e.tensor_mul(
                    out=s_dtn[:, 0:R], in0=s_rtmp[:, 0:R], in1=s_lnw[:, 0:R]),
                    reads=[b_rtmp, b_lnw], writes=[b_dtn])
                kb.op("v", lambda e: e.tensor_scalar_mul(
                    out=s_bcn[:, 0:N], in0=s_spT[:, R:R + N], scalar1=colv(1)),
                    reads=[b_spT, b_col], writes=[b_bcn])
                kb.op("v", lambda e: e.tensor_scalar_mul(
                    out=s_bcn[:, N:2 * N], in0=s_spT[:, R + N:R + 2 * N],
                    scalar1=colv(2)),
                    reads=[b_spT, b_col], writes=[b_bcn])
                kb.op("v", lambda e: e.tensor_mul(
                    out=s_bcn[:, 0:N], in0=s_bcn[:, 0:N], in1=s_lnw[:, R:R + N]),
                    reads=[b_bcn, b_lnw], writes=[b_bcn])
                kb.op("v", lambda e: e.tensor_mul(
                    out=s_bcn[:, N:2 * N], in0=s_bcn[:, N:2 * N],
                    in1=s_lnw[:, R + N:R + 2 * N]),
                    reads=[b_bcn, b_lnw], writes=[b_bcn])
                kb.op("t", lambda e: e.transpose(
                    psTf[0:2 * N, 0:128], s_bcn[:, 0:2 * N], s_idf[:]),
                    reads=[b_bcn, b_idf], writes=[b_psTf])
                kb.op("s", lambda e: e.copy(
                    out=s_bcT[0:2 * N, 0:128], in_=psTf[0:2 * N, 0:128]),
                    reads=[b_psTf], writes=[b_bcT])
                kb.dma(bnT[:, c0r + tt * 128:c0r + tt * 128 + 128],
                       s_bcT[0:N, 0:128], reads=[b_bcT], ch="s")
                kb.dma(cnT[:, c0r + tt * 128:c0r + tt * 128 + 128],
                       s_bcT[N:2 * N, 0:128], reads=[b_bcT], ch="s")
                di = tt % 2
                for ri, (r0, rw) in enumerate(rtiles):
                    kb.op("t", lambda e, r0=r0, rw=rw, ri=ri, di=di: e.transpose(
                        psTx[di][0:rw, ri * 128:ri * 128 + 128],
                        s_dtn[:, r0:r0 + rw], s_idb[:]),
                        reads=[b_dtn, b_idb], writes=[b_psTx[di]])
                for ri, (r0, rw) in enumerate(rtiles):
                    kb.op("s", lambda e, ri=ri, tt=tt, rw=rw, di=di: e.copy(
                        out=s_dtnT[:rw, ri * TC + tt * 128:ri * TC + tt * 128 + 128],
                        in_=psTx[di][0:rw, ri * 128:ri * 128 + 128]),
                        reads=[b_psTx[di]], writes=[b_dtnT])

            for db in range(nD):
                pi = db % 2
                wi = db % NW
                for ri, (r0, rw) in enumerate(rtiles):
                    kb.dma(s_wst[wi][:rw, ri * 128:ri * 128 + 128],
                           wdt_bf[ri, 0:rw, db * 128:(db + 1) * 128],
                           reads=[b_wdtbf], writes=[b_wst[wi]])

                def fn_d(e, pi=pi, wi=wi, w=w):
                    for ri, (r0, rw) in enumerate(rtiles):
                        inst = e.matmul(
                            ps[pi][:, 0:w],
                            s_wst[wi][:rw, ri * 128:ri * 128 + 128],
                            s_dtnT[:rw, ri * TC:ri * TC + w],
                            start=(ri == 0), stop=(ri == nR - 1))
                    return inst

                kb.op("t", fn_d, reads=[b_wst[wi], b_dtnT], writes=[b_ps[pi]])
                kb.op("s", lambda e, pi=pi, db=db, w=w: e.activation(
                    out=s_sf[0][:, 0:w], in_=ps[pi][:, 0:w], func=AF.Abs,
                    bias=s_dtbias[:, db:db + 1]),
                    reads=[b_ps[pi], b_dtbias], writes=[b_sf[0]])
                kb.op("s", lambda e, w=w: e.activation(
                    out=s_sf[1][:, 0:w], in_=s_sf[0][:, 0:w], func=AF.Exp,
                    scale=CST[-1.0]),
                    reads=[b_sf[0], b_cst], writes=[b_sf[1]])
                kb.op("s", lambda e, w=w: e.activation(
                    out=s_sf[0][:, 0:w], in_=s_sf[1][:, 0:w], func=AF.Ln,
                    bias=1.0),
                    reads=[b_sf[1]], writes=[b_sf[0]])
                kb.op("s", lambda e, pi=pi, db=db, w=w: e.activation(
                    out=s_sf[1][:, 0:w], in_=ps[pi][:, 0:w], func=AF.Relu,
                    bias=s_dtbias[:, db:db + 1]),
                    reads=[b_ps[pi], b_dtbias], writes=[b_sf[1]])
                kb.op("v", lambda e, w=w, pi=pi: e.tensor_add(
                    out=s_dl[pi][:, 0:w], in0=s_sf[0][:, 0:w],
                    in1=s_sf[1][:, 0:w]),
                    reads=[b_sf[0], b_sf[1]], writes=[b_dl[pi]])
                kb.dma(dout[db, :, c0r:c0r + w], s_dl[pi][:, 0:w],
                       reads=[b_dl[pi]], ch="s")

        block = es.enter_context(nc.Block())
        kb.build_and_finish(block)
    return nc


def build_L2(c: Cfg):
    T, DS, N, SEG = c.T, c.DS, c.N, c.SEG
    nb = DS // 128
    nseg = T // SEG

    nc = bass.Bass("TRN2", target_bir_lowering=False, num_devices=8)
    dt = nc.dram_tensor
    delta = dt("delta", [nb, 128, T], F32, kind="ExternalInput").ap()
    u = dt("u", [nb, 128, T], F32, kind="ExternalInput").ap()
    gate = dt("gate", [nb, 128, T], F32, kind="ExternalInput").ap()
    bnT = dt("bnT", [N, T], F32, kind="ExternalInput").ap()
    cnT = dt("cnT", [N, T], F32, kind="ExternalInput").ap()
    a_sl = dt("a_sl", [128, nb * N], F32, kind="ExternalInput").ap()
    identf = dt("identf", [128, 128], F32, kind="ExternalInput").ap()
    dpar = dt("dpar", [128, nb], F32, kind="ExternalInput").ap()
    y = dt("y", [nb, 128, T], BF16, kind="ExternalOutput").ap()

    es = contextlib.ExitStack()
    with es:
        Sb = lambda nm, shp, dtp=F32: es.enter_context(nc.sbuf_tensor(nm, shp, dtp))
        kb = KB(nc, es, engines=("v", "s", "t", "y"))

        s_A = Sb("s_A", [128, nb * N]); b_A = Buf("A")
        s_D = Sb("s_D", [128, nb]); b_D = Buf("D")
        s_hl = Sb("s_hl", [128, nb * N])
        b_hl = [Buf(f"hl{i}") for i in range(nb * N)]
        s_Bb = [Sb(f"s_Bb_{i}", [128, SEG]) for i in range(N)]
        b_Bb = [Buf(f"Bb{i}") for i in range(N)]
        s_Cb = [Sb(f"s_Cb_{i}", [128, SEG]) for i in range(N)]
        b_Cb = [Buf(f"Cb{i}") for i in range(N)]
        s_d = [Sb(f"s_d_{i}", [128, SEG]) for i in range(2)]
        b_d = [Buf(f"d{i}") for i in range(2)]
        s_u = [Sb(f"s_u_{i}", [128, SEG]) for i in range(2)]
        b_u = [Buf(f"u{i}") for i in range(2)]
        s_du = [Sb(f"s_du_{i}", [128, SEG]) for i in range(2)]
        b_du = [Buf(f"du{i}") for i in range(2)]
        s_g = [Sb(f"s_g_{i}", [128, SEG]) for i in range(2)]
        b_g = [Buf(f"g{i}") for i in range(2)]
        s_ya = [Sb(f"s_ya_{i}", [128, SEG]) for i in range(2)]
        b_ya = [Buf(f"ya{i}") for i in range(2)]
        s_dA = [Sb(f"s_dA_{i}", [128, SEG]) for i in range(2)]
        b_dA = [Buf(f"dA{i}") for i in range(2)]
        s_dB = Sb("s_dB", [128, SEG]); b_dB = Buf("dB")
        s_h = Sb("s_h", [128, SEG]); b_h = Buf("h")
        s_hc = [Sb(f"s_hc_{i}", [128, SEG]) for i in range(2)]
        b_hc = [Buf(f"hc{i}") for i in range(2)]
        NHF = ceil_div(SEG, 512)
        WHF = min(512, SEG)
        psY = [[es.enter_context(nc.psum_tensor(f"psY_{i}_{hh}", [128, 512], F32))
                for hh in range(NHF)] for i in range(2)]
        b_psY = [[Buf(f"psY{i}_{hh}") for hh in range(NHF)] for i in range(2)]
        s_yb = [Sb(f"s_yb_{i}", [128, SEG], BF16) for i in range(2)]
        b_yb = [Buf(f"yb{i}") for i in range(2)]

        s_idf = Sb("s_idf", [128, 128], F32); b_idf = Buf("idf")
        kb.dma(s_idf[:], identf, writes=[b_idf])
        kb.dma(s_A[:], a_sl, writes=[b_A])
        kb.dma(s_D[:], dpar, writes=[b_D])

        for sg in range(nseg):
            sl = slice(sg * SEG, (sg + 1) * SEG)
            for n in range(N):
                kb.dma(s_Bb[n][:], bnT[n:n + 1, sl].partition_broadcast(128),
                       writes=[b_Bb[n]])
                kb.dma(s_Cb[n][:], cnT[n:n + 1, sl].partition_broadcast(128),
                       writes=[b_Cb[n]])
            for b in range(nb):
                i = b % 2
                kb.dma(s_d[i][:], delta[b, :, sl], writes=[b_d[i]])
                kb.dma(s_u[i][:], u[b, :, sl], writes=[b_u[i]])
                kb.dma(s_g[i][:], gate[b, :, sl], writes=[b_g[i]])
                kb.op("v", lambda e, i=i: e.tensor_mul(
                    out=s_du[i][:], in0=s_d[i][:], in1=s_u[i][:]),
                    reads=[b_d[i], b_u[i]], writes=[b_du[i]])
                for n in range(N):
                    hli = b * N + n
                    ai = n % 2
                    kb.op("s", lambda e, i=i, b=b, n=n, ai=ai: e.activation(
                        out=s_dA[ai][:], in_=s_d[i][:], func=AF.Exp,
                        scale=s_A[:, b * N + n:b * N + n + 1]),
                        reads=[b_d[i], b_A], writes=[b_dA[ai]])
                    kb.op("v", lambda e, i=i, n=n: e.tensor_mul(
                        out=s_dB[:], in0=s_du[i][:], in1=s_Bb[n][:]),
                        reads=[b_du[i], b_Bb[n]], writes=[b_dB])
                    if sg == 0:
                        kb.op("v", lambda e, ai=ai: e.tensor_tensor_scan(
                            out=s_h[:], data0=s_dA[ai][:], data1=s_dB[:],
                            initial=0.0, op0=OP.mult, op1=OP.add),
                            reads=[b_dA[ai], b_dB], writes=[b_h])
                    else:
                        kb.op("v", lambda e, ai=ai, hli=hli: e.tensor_tensor_scan(
                            out=s_h[:], data0=s_dA[ai][:], data1=s_dB[:],
                            initial=s_hl[:, hli:hli + 1],
                            op0=OP.mult, op1=OP.add),
                            reads=[b_dA[ai], b_dB, b_hl[hli]], writes=[b_h])
                    if sg < nseg - 1:
                        kb.op("s", lambda e, hli=hli: e.copy(
                            out=s_hl[:, hli:hli + 1], in_=s_h[:, SEG - 1:SEG]),
                            reads=[b_h], writes=[b_hl[hli]])
                    ci = n % 2
                    kb.op("v", lambda e, n=n, ci=ci: e.tensor_mul(
                        out=s_hc[ci][:], in0=s_h[:], in1=s_Cb[n][:]),
                        reads=[b_h, b_Cb[n]], writes=[b_hc[ci]])
                    for hh in range(NHF):
                        kb.op("t", lambda e, i=i, n=n, ci=ci, hh=hh: e.matmul(
                            psY[i][hh][:, 0:WHF],
                            s_idf[:],
                            s_hc[ci][:, hh * WHF:(hh + 1) * WHF],
                            start=(n == 0), stop=(n == N - 1)),
                            reads=[b_hc[ci], b_idf], writes=[b_psY[i][hh]])
                for hh in range(NHF):
                    kb.op("v", lambda e, i=i, b=b, hh=hh: e.scalar_tensor_tensor(
                        out=s_ya[i][:, hh * WHF:(hh + 1) * WHF],
                        in0=s_u[i][:, hh * WHF:(hh + 1) * WHF],
                        scalar=s_D[:, b:b + 1],
                        in1=psY[i][hh][:, 0:WHF], op0=OP.mult, op1=OP.add),
                        reads=[b_u[i], b_D, b_psY[i][hh]], writes=[b_ya[i]])
                kb.op("v", lambda e, i=i: e.tensor_mul(
                    out=s_yb[i][:], in0=s_ya[i][:], in1=s_g[i][:]),
                    reads=[b_ya[i], b_g[i]], writes=[b_yb[i]])
                kb.dma(y[b, :, sl], s_yb[i][:], reads=[b_yb[i]], ch="s")

        block = es.enter_context(nc.Block())
        kb.build_and_finish(block)
    return nc


def build_L3(c: Cfg):
    H, D = c.H, c.D
    S, TC = c.S, c.TC
    nH, nD = H // 128, D // 128
    chunks = [(i * TC, TC) for i in range(S // TC)]

    nc = bass.Bass("TRN2", target_bir_lowering=False, num_devices=8)
    dt = nc.dram_tensor
    w_out_t = dt("w_out_t", [nH, 128, nD, 128], F32, kind="ExternalInput").ap()
    yin = dt("yin", [nD, 128, S], BF16, kind="ExternalInput").ap()
    acc = dt("acc", [nH, 128, S], F32, kind="ExternalInput").ap()
    x = dt("x", [S, H], F32, kind="ExternalInput").ap()
    identf = dt("identf", [128, 128], F32, kind="ExternalInput").ap()
    out = dt("out", [S, H], F32, kind="ExternalOutput").ap()
    w_out_bf = dt("w_out_bf", [nH, 128, nD * 128], BF16, kind="Internal").ap()
    b_wobf = [Buf(f"wobf{i}") for i in range(nH)]

    es = contextlib.ExitStack()
    with es:
        Sb = lambda nm, shp, dtp=F32: es.enter_context(nc.sbuf_tensor(nm, shp, dtp))
        kb = KB(nc, es)

        s_idf = Sb("s_idf", [128, 128], F32); b_idf = Buf()
        kb.dma(s_idf[:], identf, writes=[b_idf])

        with contextlib.ExitStack() as es0:
            s_w32 = [es0.enter_context(
                nc.sbuf_tensor(f"s_w32_{i}", [128, nD * 128], F32))
                for i in range(2)]
            b_w32 = [Buf() for _ in range(2)]
            s_wcc = [es0.enter_context(
                nc.sbuf_tensor(f"s_wcc_{i}", [128, nD * 128], BF16))
                for i in range(2)]
            b_wcc = [Buf() for _ in range(2)]
            for hb in range(nH):
                i = hb % 2
                kb.dma(s_w32[i][:], w_out_t[hb].rearrange("p db oc -> p (db oc)"),
                       writes=[b_w32[i]])
                kb.op("s", lambda e, i=i: e.copy(out=s_wcc[i][:], in_=s_w32[i][:]),
                      reads=[b_w32[i]], writes=[b_wcc[i]])
                kb.dma(w_out_bf[hb], s_wcc[i][:], reads=[b_wcc[i]],
                       writes=[b_wobf[hb]], sem_buf=b_wcc[i], ch="s")

        kb.fence()
        s_yall = Sb("s_yall", [128, nD * TC], BF16); b_yall = Buf()
        s_wst = [Sb(f"s_wst_{i}", [128, nD * 128], BF16) for i in range(2)]
        b_wst = [Buf() for _ in range(2)]
        s_acc = Sb("s_acc", [128, TC]); b_acc = Buf()
        s_oh = Sb("s_oh", [128, TC]); b_oh = Buf()
        s_fin = [Sb(f"s_fin_{i}", [128, H]) for i in range(TC // 128)]
        b_fin = [Buf() for _ in range(TC // 128)]
        s_x = Sb("s_x", [128, H]); b_x = Buf()

        ps = [es.enter_context(nc.psum_tensor(f"ps_{i}", [128, 512], F32))
              for i in range(2)]
        b_ps = [Buf() for _ in range(2)]
        psT3 = [es.enter_context(nc.psum_tensor(f"psT3_{i}", [128, 128], F32))
                for i in range(2)]
        b_psT3 = [Buf(f"psT3{i}") for i in range(2)]

        for (c0, w) in chunks:
            for db in range(nD):
                kb.dma(s_yall[:, db * TC:db * TC + w], yin[db, :, c0:c0 + w],
                       writes=[b_yall])
            for hb in range(nH):
                pi = hb % 2
                kb.dma(s_wst[pi][:], w_out_bf[hb],
                       reads=[b_wobf[hb]], writes=[b_wst[pi]])

                def fn_o(e, pi=pi, w=w):
                    for db in range(nD):
                        inst = e.matmul(
                            ps[pi][:, 0:w],
                            s_wst[pi][:, db * 128:(db + 1) * 128],
                            s_yall[:, db * TC:db * TC + w],
                            start=(db == 0), stop=(db == nD - 1))
                    return inst

                kb.op("t", fn_o, reads=[b_wst[pi], b_yall], writes=[b_ps[pi]])
                kb.dma(s_acc[:, 0:w], acc[hb, :, c0:c0 + w], writes=[b_acc])
                kb.op("v", lambda e, pi=pi, w=w: e.tensor_add(
                    out=s_oh[:, 0:w], in0=ps[pi][:, 0:w], in1=s_acc[:, 0:w]),
                    reads=[b_ps[pi], b_acc], writes=[b_oh])
                for tt in range(w // 128):
                    zi = tt % 2
                    kb.op("t", lambda e, tt=tt, zi=zi: e.transpose(
                        psT3[zi][:], s_oh[:, tt * 128:(tt + 1) * 128], s_idf[:]),
                        reads=[b_oh, b_idf], writes=[b_psT3[zi]])
                    kb.op("s", lambda e, hb=hb, tt=tt, zi=zi: e.copy(
                        out=s_fin[tt][:, hb * 128:(hb + 1) * 128],
                        in_=psT3[zi][:]),
                        reads=[b_psT3[zi]], writes=[b_fin[tt]])
            for tt in range(w // 128):
                t0 = c0 + tt * 128
                kb.dma(s_x[:], x[t0:t0 + 128, :], writes=[b_x])
                kb.op("v", lambda e, tt=tt: e.tensor_add(
                    out=s_fin[tt][:], in0=s_fin[tt][:], in1=s_x[:]),
                    reads=[b_x], writes=[b_fin[tt]])
                kb.dma(out[t0:t0 + 128, :], s_fin[tt][:], reads=[b_fin[tt]],
                       ch="s")

        block = es.enter_context(nc.Block())
        kb.build_and_finish(block)
    return nc


def host_prepare(c: Cfg, inputs):
    H, D, R, N, K, G, GS = c.H, c.D, c.R, c.N, c.K, c.G, c.GS
    S, HB = c.S, c.HALO
    nH, nD, nO = H // 128, D // 128, H // 128

    xfull = np.asarray(inputs["L0_attn_hidden_states"], np.float32)
    B, T, _ = xfull.shape
    xf = xfull.reshape(B * T, H)
    W_in = np.asarray(inputs["L0_attn_W_in"], np.float32)
    w_in_t = np.ascontiguousarray(
        W_in.reshape(2 * D // 128, 128, nH, 128).transpose(0, 3, 2, 1))
    conv_w = np.ascontiguousarray(
        np.asarray(inputs["L0_attn_conv_w"], np.float32)
        .reshape(nD, 128, K).transpose(1, 0, 2).reshape(128, nD * K))
    conv_b = np.ascontiguousarray(
        np.asarray(inputs["L0_attn_conv_b"], np.float32).reshape(nD, 128).T)
    wxT = np.ascontiguousarray(np.asarray(inputs["L0_attn_Wx"], np.float32).T)
    ln_w_cat = np.concatenate([
        np.asarray(inputs["L0_attn_dt_ln_w"], np.float32),
        np.asarray(inputs["L0_attn_b_ln_w"], np.float32),
        np.asarray(inputs["L0_attn_c_ln_w"], np.float32)]).reshape(1, -1)
    wdtT = np.ascontiguousarray(np.asarray(inputs["L0_attn_Wdt"], np.float32).T)
    dt_bias = np.ascontiguousarray(
        np.asarray(inputs["L0_attn_dt_bias"], np.float32).reshape(nD, 128).T)
    qw = np.ascontiguousarray(
        np.asarray(inputs["L0_ffn_qweight"], np.int32).reshape(H, H))
    qscale = np.ascontiguousarray(
        np.asarray(inputs["L0_ffn_scales"], np.float32)
        .reshape(nO, 128, G).transpose(1, 0, 2).reshape(128, nO * G))
    qzero = np.ascontiguousarray(
        np.asarray(inputs["L0_ffn_scaled_zeros"], np.float32)
        .reshape(nO, 128, G).transpose(1, 0, 2).reshape(128, nO * G))
    fbias = np.ascontiguousarray(
        np.asarray(inputs["L0_ffn_bias"], np.float32).reshape(nO, 128).T)
    import ml_dtypes
    identb = np.eye(128, dtype=ml_dtypes.bfloat16)
    identf = np.eye(128, dtype=np.float32)

    in_maps1 = []
    for cc in range(NCORES):
        t0 = cc * S
        xe = np.zeros((HB + S, H), np.float32)
        xe[HB:] = xf[t0:t0 + S]
        bstart = (t0 // T) * T
        nh = min(HB, t0 - bstart)
        if nh > 0:
            xe[HB - nh:HB] = xf[t0 - nh:t0]
        in_maps1.append(dict(
            x=xe, w_in_t=w_in_t, conv_w=conv_w, conv_b=conv_b, wxT=wxT,
            ln_w_cat=ln_w_cat, wdtT=wdtT, dt_bias=dt_bias, qw=qw,
            qscale=qscale, qzero=qzero, fbias=fbias, identb=identb,
            identf=identf))
    return in_maps1


def host_l2_maps(c: Cfg, r1):
    H, D, N = c.H, c.D, c.N
    S = c.S
    nD = D // 128
    nbat = NCORES // c.B
    DS = c.DS
    nb = DS // 128

    def cat_t(key, nblk):
        outs = []
        for b in range(c.B):
            cores = [r1[b * nbat + i][key] for i in range(nbat)]
            outs.append(np.concatenate(cores, axis=2))
        return outs

    dlt = cat_t("dout", nD)
    uu = cat_t("uout", nD)
    gg = cat_t("gout", nD)
    bn = cat_t("bnT", None)
    cn = cat_t("cnT", None)
    A = np.asarray(None) if False else None
    in_maps2 = []
    for cc in range(NCORES):
        b = cc // (NCORES // c.B)
        q = cc % (NCORES // c.B)
        sl = slice(q * nb, (q + 1) * nb)
        in_maps2.append(dict(
            delta=np.ascontiguousarray(dlt[b][sl]),
            u=np.ascontiguousarray(uu[b][sl]),
            gate=np.ascontiguousarray(gg[b][sl]),
            bnT=np.ascontiguousarray(bn[b]),
            cnT=np.ascontiguousarray(cn[b]),
        ))
    return in_maps2


def run_all(c: Cfg, inputs, runner):
    import ml_dtypes
    H, D, N = c.H, c.D, c.N
    S = c.S
    nD, nH = D // 128, H // 128
    nbat = NCORES // c.B
    DS = c.DS
    nb = DS // 128

    in1 = host_prepare(c, inputs)
    nc1 = build_L1(c)
    r1 = runner(nc1, in1)

    A = np.asarray(inputs["L0_attn_A"], np.float32).reshape(nD, 128, N)
    Dp = np.asarray(inputs["L0_attn_D_param"], np.float32).reshape(nD, 128)
    in2 = []
    per_b = {}
    for b in range(c.B):
        cores = [r1[b * nbat + i] for i in range(nbat)]
        per_b[b] = dict(
            delta=np.concatenate([r["dout"] for r in cores], axis=2),
            u=np.concatenate([r["uout"] for r in cores], axis=2),
            gate=np.concatenate([r["gout"] for r in cores], axis=2),
            bnT=np.concatenate([r["bnT"] for r in cores], axis=1),
            cnT=np.concatenate([r["cnT"] for r in cores], axis=1),
        )
    for cc in range(NCORES):
        b, q = cc // nbat, cc % nbat
        sl = slice(q * nb, (q + 1) * nb)
        in2.append(dict(
            identf=np.eye(128, dtype=np.float32),
            delta=np.ascontiguousarray(per_b[b]["delta"][sl]),
            u=np.ascontiguousarray(per_b[b]["u"][sl]),
            gate=np.ascontiguousarray(per_b[b]["gate"][sl]),
            bnT=per_b[b]["bnT"], cnT=per_b[b]["cnT"],
            a_sl=np.ascontiguousarray(
                A[sl].transpose(1, 0, 2).reshape(128, nb * N)),
            dpar=np.ascontiguousarray(Dp[sl].T)))
    nc2 = build_L2(c)
    r2 = runner(nc2, in2)

    W_out = np.asarray(inputs["L0_attn_W_out"], np.float32)
    w_out_t = np.ascontiguousarray(
        W_out.reshape(nH, 128, nD, 128).transpose(0, 3, 2, 1))
    identf = np.eye(128, dtype=np.float32)
    y_b = {}
    for b in range(c.B):
        y_b[b] = np.concatenate(
            [r2[b * nbat + q]["y"] for q in range(nbat)], axis=0)
    xfull = np.asarray(inputs["L0_attn_hidden_states"], np.float32)
    xf = xfull.reshape(c.B * c.T, H)
    in3 = []
    for cc in range(NCORES):
        t0 = cc * S
        b = t0 // c.T
        tb = t0 - b * c.T
        in3.append(dict(
            w_out_t=w_out_t,
            yin=np.ascontiguousarray(y_b[b][:, :, tb:tb + S]),
            acc=r1[cc]["aout"],
            x=np.ascontiguousarray(xf[t0:t0 + S]),
            identf=identf))
    nc3 = build_L3(c)
    r3 = runner(nc3, in3)

    out = np.concatenate([r3[cc]["out"] for cc in range(NCORES)], axis=0)
    return out.reshape(c.B, c.T, H)


from concourse.bass_utils import run_bass_kernel_spmd


def _hw_runner(nc, in_maps):
    res = run_bass_kernel_spmd(nc, in_maps, core_ids=list(range(NCORES)))
    return res.results


def kernel(**inputs):
    c = Cfg()
    out = run_all(c, inputs, _hw_runner)
    return np.ascontiguousarray(out.astype(np.float32))


# version 05
# speedup vs baseline: 1.1087x; 1.0152x over previous
import contextlib
from dataclasses import dataclass

import numpy as np
import concourse.bass as bass
import concourse.mybir as mybir

F32 = mybir.dt.float32
BF16 = mybir.dt.bfloat16
I32 = mybir.dt.int32
AF = mybir.ActivationFunctionType
OP = mybir.AluOpType
EPS_LN = 1e-5
EPS_RMS = 1e-6
NCORES = 8

ENG_ATTR = {"v": "vector", "s": "scalar", "t": "tensor", "g": "gpsimd", "y": "sync"}


def ceil_div(a, b):
    return (a + b - 1) // b


@dataclass
class Cfg:
    B: int = 2
    T: int = 8192
    H: int = 4096
    D: int = 8192
    R: int = 256
    N: int = 16
    K: int = 4
    G: int = 32
    GS: int = 128
    HALO: int = 128
    TC: int = 512
    SEG: int = 1024
    NG: int = 4

    @property
    def S(self):
        return self.B * self.T // NCORES

    @property
    def DS(self):
        return self.D // (NCORES // self.B)


class Buf:
    __slots__ = ("name", "w", "r", "sem")

    def __init__(self, name=""):
        self.name = name
        self.w = None
        self.r = []
        self.sem = None


class KB:

    def __init__(self, nc, es, engines=("v", "s", "t", "y")):
        self.nc = nc
        self.es = es
        self.engines = engines
        self.sem = {e: es.enter_context(nc.semaphore("sem_" + e)) for e in engines}
        self.cnt = {e: 0 for e in engines}
        self.plan = {e: [] for e in engines}
        self.known = {e: {} for e in engines}
        self.nsem = 0
        self.fence_w = {}

    def _buf_sem(self, b):
        if b.sem is None:
            b.sem = "bs%d_%s" % (self.nsem, b.name)
            self.nsem += 1
            self.sem[b.sem] = self.es.enter_context(
                self.nc.semaphore(b.sem))
            self.cnt[b.sem] = 0
        return b.sem

    def fence(self):
        for s, cc in self.cnt.items():
            if cc > 0:
                self.fence_w[s] = cc

    def op(self, eng, fn, reads=(), writes=(), inc=1, dma_buf=None):
        waits = dict(self.fence_w)
        for b in reads:
            if b.w:
                waits[b.w[0]] = max(waits.get(b.w[0], 0), b.w[1])
        for b in writes:
            if b.w:
                waits[b.w[0]] = max(waits.get(b.w[0], 0), b.w[1])
            for t in b.r:
                waits[t[0]] = max(waits.get(t[0], 0), t[1])
        kn = self.known[eng]
        wl = []
        for e2, cc in sorted(waits.items()):
            if cc > kn.get(e2, 0):
                wl.append((e2, cc))
                kn[e2] = cc
        if dma_buf is not None:
            sname = self._buf_sem(dma_buf)
        else:
            sname = eng
        self.cnt[sname] += inc
        tok = (sname, self.cnt[sname])
        self.plan[eng].append((wl, fn, inc, sname))
        for b in writes:
            b.w = tok
            b.r = []
        for b in reads:
            b.r.append(tok)
        return tok

    def dma(self, out, in_, reads=(), writes=(), ch="y", sem_buf=None):
        if sem_buf is None:
            sem_buf = writes[0] if writes else reads[0]
        return self.op(ch, lambda e: e.dma_start(out=out, in_=in_),
                       reads, writes, inc=16, dma_buf=sem_buf)

    def build_and_finish(self, block):
        sem = self.sem
        for e in self.engines:
            plan = self.plan[e]

            def body(eng, plan=plan, e=e):
                for wl, fn, inc, sname in plan:
                    for (we, wc) in wl:
                        eng.wait_ge(sem[we], wc)
                    fn(eng).then_inc(sem[sname], inc)
                if e == "y":
                    for s2, cc in self.cnt.items():
                        if cc > 0:
                            eng.wait_ge(sem[s2], cc)

            getattr(block, ENG_ATTR[e])(body)


def build_L1(c: Cfg):
    H, D, R, N, K = c.H, c.D, c.R, c.N, c.K
    S, HB, TC = c.S, c.HALO, c.TC
    SE = S + HB
    G, GS = c.G, c.GS
    O, I = H, H
    nH, nD, nO, nI = H // 128, D // 128, H // 128, H // 128
    RC = R + 2 * N
    rtiles = [(i * 128, min(128, R - i * 128)) for i in range(ceil_div(R, 128))]
    nR = len(rtiles)
    ogs = [(i * 128, min(128, RC - i * 128)) for i in range(ceil_div(RC, 128))]
    chunks = [(0, HB, True)] + [(HB + i * TC, TC, False) for i in range(S // TC)]

    nc = bass.Bass("TRN2", target_bir_lowering=False, num_devices=8)
    dt = nc.dram_tensor
    x = dt("x", [SE, H], F32, kind="ExternalInput").ap()
    w_in_t = dt("w_in_t", [2 * D // 128, 128, nH, 128], F32, kind="ExternalInput").ap()
    conv_w = dt("conv_w", [128, nD * K], F32, kind="ExternalInput").ap()
    conv_b = dt("conv_b", [128, nD], F32, kind="ExternalInput").ap()
    wxT = dt("wxT", [D, RC], F32, kind="ExternalInput").ap()
    ln_w_cat = dt("ln_w_cat", [1, RC], F32, kind="ExternalInput").ap()
    wdtT = dt("wdtT", [R, D], F32, kind="ExternalInput").ap()
    dt_bias = dt("dt_bias", [128, nD], F32, kind="ExternalInput").ap()
    qw = dt("qw", [O, I], I32, kind="ExternalInput").ap()
    qscale = dt("qscale", [128, nO * G], F32, kind="ExternalInput").ap()
    qzero = dt("qzero", [128, nO * G], F32, kind="ExternalInput").ap()
    fbias = dt("fbias", [128, nO], F32, kind="ExternalInput").ap()
    identb = dt("identb", [128, 128], BF16, kind="ExternalInput").ap()
    identf = dt("identf", [128, 128], F32, kind="ExternalInput").ap()

    dout = dt("dout", [nD, 128, S], F32, kind="ExternalOutput").ap()
    uout = dt("uout", [nD, 128, S], F32, kind="ExternalOutput").ap()
    gout = dt("gout", [nD, 128, S], F32, kind="ExternalOutput").ap()
    aout = dt("aout", [nO, 128, S], F32, kind="ExternalOutput").ap()
    bnT = dt("bnT", [N, S], F32, kind="ExternalOutput").ap()
    cnT = dt("cnT", [N, S], F32, kind="ExternalOutput").ap()

    w_in_bf = dt("w_in_bf", [2 * D // 128, 128, nH * 128], BF16, kind="Internal").ap()
    wdt_bf = dt("wdt_bf", [nR, 128, D], BF16, kind="Internal").ap()
    wx_bf = dt("wx_bf", [nD, 128, RC], BF16, kind="Internal").ap()
    b_wxbf = [Buf(f"wxbf{i}") for i in range(nD)]
    wT_scr = dt("wT_scr", [nO, 128, nI, 128], BF16, kind="Internal").ap()
    u_bf_scr = dt("u_bf_scr", [nD, 128, S], BF16, kind="Internal").ap()
    b_winbf = [Buf(f"winbf{i}") for i in range(2 * D // 128)]
    b_wdtbf = Buf("wdtbf")
    b_wtscr = [Buf(f"wtscr{i}") for i in range(nO)]
    b_uscr = [Buf(f"uscr{i}") for i in range(nD)]

    es = contextlib.ExitStack()
    with es:
        Sb = lambda nm, shp, dtp=F32: es.enter_context(nc.sbuf_tensor(nm, shp, dtp))
        kb = KB(nc, es)

        s_convw = Sb("s_convw", [128, nD * K]); b_convw = Buf()
        s_convb = Sb("s_convb", [128, nD]); b_convb = Buf()
        s_dtbias = Sb("s_dtbias", [128, nD]); b_dtbias = Buf()
        s_fbias = Sb("s_fbias", [128, nO]); b_fbias = Buf()
        s_lnw = Sb("s_lnw", [128, RC]); b_lnw = Buf()
        s_qsc = Sb("s_qsc", [128, nO * G]); b_qsc = Buf()
        s_qzr = Sb("s_qzr", [128, nO * G]); b_qzr = Buf()
        s_idb = Sb("s_idb", [128, 128], BF16); b_idb = Buf()
        s_idf = Sb("s_idf", [128, 128], F32); b_idf = Buf()
        s_tail = Sb("s_tail", [128, nD * (K - 1)])
        b_tail = [Buf() for _ in range(nD)]
        s_cst = Sb("s_cst", [128, 8]); b_cst = Buf()
        kb.op("v", lambda e: e.memset(s_tail[:], 0.0),
              writes=b_tail)
        CST = {}
        for ci, cv in enumerate([EPS_LN, EPS_RMS, -0.5, -1.0, 1.0 / R, 1.0 / N]):
            kb.op("v", lambda e, ci=ci, cv=cv: e.memset(s_cst[:, ci:ci + 1], cv),
                  writes=[b_cst])
            CST[cv] = s_cst[:, ci:ci + 1]

        kb.dma(s_convw[:], conv_w, writes=[b_convw])
        kb.dma(s_convb[:], conv_b, writes=[b_convb])
        kb.dma(s_dtbias[:], dt_bias, writes=[b_dtbias])
        kb.dma(s_fbias[:], fbias, writes=[b_fbias])
        kb.dma(s_lnw[:], ln_w_cat.partition_broadcast(128), writes=[b_lnw])
        kb.dma(s_qsc[:], qscale, writes=[b_qsc])
        kb.dma(s_qzr[:], qzero, writes=[b_qzr])
        kb.dma(s_idb[:], identb, writes=[b_idb])
        kb.dma(s_idf[:], identf, writes=[b_idf])

        with contextlib.ExitStack() as es0:
            S0 = lambda nm, shp, dtp=F32: es0.enter_context(
                nc.sbuf_tensor(nm, shp, dtp))
            s_w32 = [S0(f"s_w32_{i}", [128, nH * 128]) for i in range(2)]
            b_w32 = [Buf() for _ in range(2)]
            s_wc = [S0(f"s_wc_{i}", [128, nH * 128], BF16) for i in range(2)]
            b_wc = [Buf() for _ in range(2)]
            s_q = [S0(f"s_q_{i}", [128, I], I32) for i in range(2)]
            b_q = [Buf() for _ in range(2)]
            s_dq = [S0(f"s_dq_{i}", [128, I], BF16) for i in range(2)]
            b_dq = [Buf() for _ in range(2)]
            s_tp = S0("s_tp", [128, 512], BF16)
            b_tp = Buf()
            ps0 = [es0.enter_context(nc.psum_tensor(f"ps0_{i}", [128, 128], BF16))
                   for i in range(2)]
            b_ps0 = [Buf() for _ in range(2)]

            for db in range(nD):
                i = db % 2
                kb.dma(s_w32[i][:, 0:RC], wxT[db * 128:(db + 1) * 128, :],
                       writes=[b_w32[i]])
                kb.op("s", lambda e, i=i: e.copy(
                    out=s_wc[i][:, 0:RC], in_=s_w32[i][:, 0:RC]),
                    reads=[b_w32[i]], writes=[b_wc[i]])
                kb.dma(wx_bf[db], s_wc[i][:, 0:RC], reads=[b_wc[i]],
                       writes=[b_wxbf[db]], sem_buf=b_wc[i], ch="s")
            nwj = ceil_div(D, nH * 128)
            for ri, (r0, rw) in enumerate(rtiles):
                for j in range(nwj):
                    w = min(nH * 128, D - j * nH * 128)
                    i = (ri * nwj + j) % 2
                    kb.dma(s_w32[i][:rw, 0:w],
                           wdtT[r0:r0 + rw, j * nH * 128:j * nH * 128 + w],
                           writes=[b_w32[i]])
                    kb.op("s", lambda e, i=i, rw=rw, w=w: e.copy(
                        out=s_wc[i][:rw, 0:w], in_=s_w32[i][:rw, 0:w]),
                        reads=[b_w32[i]], writes=[b_wc[i]])
                    kb.dma(wdt_bf[ri, :rw, j * nH * 128:j * nH * 128 + w],
                           s_wc[i][:rw, 0:w], reads=[b_wc[i]],
                           writes=[b_wdtbf], sem_buf=b_wc[i], ch="s")
            for ob in range(nO):
                i = ob % 2
                kb.dma(s_q[i][:], qw[ob * 128:(ob + 1) * 128, :], writes=[b_q[i]])
                for g in range(G):
                    kb.op("v", lambda e, i=i, ob=ob, g=g: e.tensor_scalar(
                        out=s_dq[i][:, g * GS:(g + 1) * GS],
                        in0=s_q[i][:, g * GS:(g + 1) * GS],
                        scalar1=s_qsc[:, ob * G + g:ob * G + g + 1],
                        scalar2=s_qzr[:, ob * G + g:ob * G + g + 1],
                        op0=OP.mult, op1=OP.add),
                        reads=[b_q[i], b_qsc, b_qzr], writes=[b_dq[i]])
                for ib in range(nI):
                    p = ib % 2
                    kb.op("t", lambda e, i=i, ib=ib, p=p: e.transpose(
                        ps0[p][:], s_dq[i][:, ib * 128:(ib + 1) * 128], s_idb[:]),
                        reads=[b_dq[i], b_idb], writes=[b_ps0[p]])
                    kb.op("s", lambda e, ib=ib, p=p: e.copy(
                        out=s_tp[:, (ib % 4) * 128:(ib % 4) * 128 + 128],
                        in_=ps0[p][:]),
                        reads=[b_ps0[p]], writes=[b_tp])
                    if ib % 4 == 3 or ib == nI - 1:
                        lo = (ib // 4) * 4
                        cnt = ib - lo + 1
                        kb.dma(wT_scr[ob, :, lo:lo + cnt, :]
                               .rearrange("p ib o -> p (ib o)"),
                               s_tp[:, 0:cnt * 128], reads=[b_tp],
                               writes=[b_wtscr[ob]], sem_buf=b_tp, ch="s")

        kb.fence()

        s_x = Sb("s_x", [128, H]); b_x = Buf()
        s_junk = Sb("s_junk", [128, max(H, nH * 128)]); b_junk = Buf()
        s_col = Sb("s_col", [128, 8]); b_col = Buf()
        ntt_max = max(HB, TC) // 128
        s_xnb = Sb("s_xnb", [128, ntt_max * H], BF16)
        b_xnb = [Buf(f"xnb{i}") for i in range(ntt_max)]
        s_xnT = Sb("s_xnT", [128, nH * TC], BF16); b_xnT = Buf()
        NW = 3
        s_wst = [Sb(f"s_wst_{i}", [128, max(nH, nI, nR) * 128], BF16)
                 for i in range(NW)]
        b_wst = [Buf(f"wst{i}") for i in range(NW)]
        s_uraw = [Sb(f"s_uraw_{i}", [128, TC + K - 1]) for i in range(2)]
        b_uraw = [Buf() for _ in range(2)]
        s_cacc = [Sb(f"s_cacc_{i}", [128, TC]) for i in range(4)]
        b_cacc = [Buf(f"cacc{i}") for i in range(4)]
        s_uf = [Sb(f"s_uf_{i}", [128, TC]) for i in range(2)]
        b_uf = [Buf(f"uf{i}") for i in range(2)]
        s_ub = [Sb(f"s_ub_{i}", [128, TC], BF16) for i in range(2)]
        b_ub = [Buf(f"ub{i}") for i in range(2)]
        s_gt = [Sb(f"s_gt_{i}", [128, TC]) for i in range(2)]
        b_gt = [Buf(f"gt{i}") for i in range(2)]
        s_ac = [Sb(f"s_ac_{i}", [128, TC]) for i in range(2)]
        b_ac = [Buf(f"ac{i}") for i in range(2)]
        s_ubf = [Sb(f"s_ubf_{i}", [128, TC], BF16) for i in range(2)]
        b_ubf = [Buf() for _ in range(2)]
        s_sp = Sb("s_sp", [128, len(ogs) * TC]); b_sp = Buf()
        s_spT = [Sb(f"s_spT_{i}", [128, RC]) for i in range(2)]
        b_spT = [Buf(f"spT{i}") for i in range(2)]
        s_rcol = Sb("s_rcol", [128, 8]); b_rcol = Buf()
        s_rtmp = Sb("s_rtmp", [128, R]); b_rtmp = Buf()
        s_dtn = [Sb(f"s_dtn_{i}", [128, R], BF16) for i in range(2)]
        b_dtn = [Buf(f"dtn{i}") for i in range(2)]
        s_bcn = Sb("s_bcn", [128, 2 * N]); b_bcn = Buf()
        s_bcT = Sb("s_bcT", [128, 128]); b_bcT = Buf()
        s_dtnT = Sb("s_dtnT", [128, nR * TC], BF16); b_dtnT = Buf()
        s_sf = [Sb(f"s_sf_{i}", [128, TC]) for i in range(2)]
        b_sf = [Buf() for _ in range(2)]
        s_dl = [Sb(f"s_dl_{i}", [128, TC]) for i in range(2)]
        b_dl = [Buf(f"dl{i}") for i in range(2)]

        ps = [es.enter_context(nc.psum_tensor(f"ps_{i}", [128, 512], F32))
              for i in range(4)]
        b_ps = [Buf() for _ in range(4)]
        psTx = [es.enter_context(nc.psum_tensor(f"psTx{i}", [128, 512], BF16))
                for i in range(2)]
        b_psTx = [Buf(f"psTx{i}") for i in range(2)]
        psTf = es.enter_context(nc.psum_tensor("psTf", [128, 512], F32))
        b_psTf = Buf()

        colv = lambda j: s_col[:, j:j + 1]
        win_cast_done = [False] * (2 * D // 128)

        def load_win_slab(db, wi):
            if win_cast_done[db]:
                kb.dma(s_wst[wi][:, 0:nH * 128], w_in_bf[db],
                       reads=[b_winbf[db]], writes=[b_wst[wi]])
                return
            win_cast_done[db] = True
            stg, b_stg = ((s_junk, b_junk) if db % 2 == 0 else (s_x, b_x))
            kb.dma(stg[:, 0:nH * 128],
                   w_in_t[db].rearrange("p hb oc -> p (hb oc)"),
                   writes=[b_stg], ch=("y" if db % 2 == 0 else "s"))
            kb.op("s", lambda e, wi=wi, stg=stg: e.copy(
                out=s_wst[wi][:, 0:nH * 128], in_=stg[:, 0:nH * 128]),
                reads=[b_stg], writes=[b_wst[wi]])
            kb.dma(w_in_bf[db], s_wst[wi][:, 0:nH * 128], reads=[b_wst[wi]],
                   writes=[b_winbf[db]], sem_buf=b_wst[wi], ch="s")

        for (c0, w, halo) in chunks:
            c0r = c0 - HB
            ntt = w // 128
            for tt in range(ntt):
                t0 = c0 + tt * 128
                kb.dma(s_x[:], x[t0:t0 + 128, :], writes=[b_x])
                kb.op("s", lambda e: e.activation(
                    out=s_junk[:], in_=s_x[:], func=AF.Copy, accum_out=colv(0)),
                    reads=[b_x], writes=[b_junk, b_col])
                kb.op("s", lambda e: e.activation(
                    out=s_junk[:], in_=s_x[:], func=AF.Square, accum_out=colv(1)),
                    reads=[b_x], writes=[b_junk, b_col])
                kb.op("v", lambda e: e.tensor_scalar_mul(
                    out=colv(2), in0=colv(0), scalar1=1.0 / H),
                    reads=[b_col], writes=[b_col])
                kb.op("v", lambda e: e.tensor_scalar_mul(
                    out=colv(3), in0=colv(1), scalar1=1.0 / H),
                    reads=[b_col], writes=[b_col])
                kb.op("v", lambda e: e.tensor_mul(
                    out=colv(4), in0=colv(2), in1=colv(2)),
                    reads=[b_col], writes=[b_col])
                kb.op("v", lambda e: e.tensor_sub(
                    out=colv(3), in0=colv(3), in1=colv(4)),
                    reads=[b_col], writes=[b_col])
                kb.op("s", lambda e: e.activation(
                    out=colv(4), in_=colv(3), func=AF.Ln, bias=CST[EPS_LN]),
                    reads=[b_col, b_cst], writes=[b_col])
                kb.op("s", lambda e: e.activation(
                    out=colv(4), in_=colv(4), func=AF.Exp, scale=CST[-0.5]),
                    reads=[b_col, b_cst], writes=[b_col])
                kb.op("v", lambda e: e.scalar_tensor_tensor(
                    out=colv(5), in0=colv(2), scalar=-1.0, in1=colv(4),
                    op0=OP.mult, op1=OP.mult),
                    reads=[b_col], writes=[b_col])
                kb.op("v", lambda e, tt=tt: e.tensor_scalar(
                    out=s_xnb[:, tt * H:(tt + 1) * H], in0=s_x[:],
                    scalar1=colv(4), scalar2=colv(5),
                    op0=OP.mult, op1=OP.add),
                    reads=[b_x, b_col], writes=[b_xnb[tt]])
            for hb in range(nH):
                xi = hb % 2
                for tt in range(ntt):
                    kb.op("t", lambda e, hb=hb, tt=tt, xi=xi: e.transpose(
                        psTx[xi][:, tt * 128:(tt + 1) * 128],
                        s_xnb[:, tt * H + hb * 128:tt * H + hb * 128 + 128],
                        s_idb[:]),
                        reads=[b_xnb[tt], b_idb], writes=[b_psTx[xi]])
                kb.op("s", lambda e, hb=hb, xi=xi, w=w: e.copy(
                    out=s_xnT[:, hb * TC:hb * TC + w], in_=psTx[xi][:, 0:w]),
                    reads=[b_psTx[xi]], writes=[b_xnT])

            load_win_slab(0, 0)
            load_win_slab(1, 1)
            for db in range(nD):
                pi = db % 2
                wi = db % NW
                if db + 2 < nD:
                    load_win_slab(db + 2, (db + 2) % NW)
                elif db + 2 < nD + 2 and not halo:
                    load_win_slab(nD + db + 2 - nD, (db + 2) % NW)

                def fn_u(e, pi=pi, wi=wi, w=w):
                    for hb in range(nH):
                        inst = e.matmul(
                            ps[pi][:, 0:w],
                            s_wst[wi][:, hb * 128:(hb + 1) * 128],
                            s_xnT[:, hb * TC:hb * TC + w],
                            start=(hb == 0), stop=(hb == nH - 1))
                    return inst

                kb.op("t", fn_u, reads=[b_wst[wi], b_xnT], writes=[b_ps[pi]])
                ui = db % 2
                kb.op("s", lambda e, ui=ui, db=db: e.copy(
                    out=s_uraw[ui][:, 0:K - 1],
                    in_=s_tail[:, db * (K - 1):(db + 1) * (K - 1)]),
                    reads=[b_tail[db]], writes=[b_uraw[ui]])
                kb.op("s", lambda e, ui=ui, pi=pi, w=w: e.copy(
                    out=s_uraw[ui][:, K - 1:K - 1 + w], in_=ps[pi][:, 0:w]),
                    reads=[b_ps[pi]], writes=[b_uraw[ui]])
                kb.op("s", lambda e, ui=ui, db=db, w=w: e.copy(
                    out=s_tail[:, db * (K - 1):(db + 1) * (K - 1)],
                    in_=s_uraw[ui][:, w:w + K - 1]),
                    reads=[b_uraw[ui]], writes=[b_tail[db]])
                if halo:
                    continue
                cb = 2 * (db % 2)
                kb.op("v", lambda e, ui=ui, db=db, w=w, cb=cb: e.tensor_scalar_mul(
                    out=s_cacc[cb][:, 0:w], in0=s_uraw[ui][:, 0:w],
                    scalar1=s_convw[:, db * K:db * K + 1]),
                    reads=[b_uraw[ui], b_convw], writes=[b_cacc[cb]])
                for k in range(1, K):
                    kb.op("v", lambda e, ui=ui, db=db, k=k, w=w, cb=cb:
                          e.scalar_tensor_tensor(
                              out=s_cacc[cb + k % 2][:, 0:w],
                              in0=s_uraw[ui][:, k:k + w],
                              scalar=s_convw[:, db * K + k:db * K + k + 1],
                              in1=s_cacc[cb + (k - 1) % 2][:, 0:w],
                              op0=OP.mult, op1=OP.add),
                          reads=[b_uraw[ui], b_convw, b_cacc[cb + (k - 1) % 2]],
                          writes=[b_cacc[cb + k % 2]])
                fin = cb + (K - 1) % 2
                kb.op("s", lambda e, db=db, fin=fin, w=w, ui=ui: e.activation(
                    out=s_uf[ui][:, 0:w], in_=s_cacc[fin][:, 0:w], func=AF.Silu,
                    bias=s_convb[:, db:db + 1]),
                    reads=[b_cacc[fin], b_convb], writes=[b_uf[ui]])
                kb.op("s", lambda e, w=w, ui=ui: e.copy(
                    out=s_ub[ui][:, 0:w], in_=s_uf[ui][:, 0:w]),
                    reads=[b_uf[ui]], writes=[b_ub[ui]])
                kb.dma(uout[db, :, c0r:c0r + w], s_uf[ui][:, 0:w],
                       reads=[b_uf[ui]], ch="s")
                kb.dma(u_bf_scr[db, :, c0r:c0r + w], s_ub[ui][:, 0:w],
                       reads=[b_ub[ui]], writes=[b_uscr[db]], sem_buf=b_ub[ui],
                       ch="s")

            if halo:
                continue

            for db in range(nD):
                pi = db % 2
                wi = (nD + db) % NW
                if db + 2 < nD:
                    load_win_slab(nD + db + 2, (nD + db + 2) % NW)

                def fn_g(e, pi=pi, wi=wi, w=w):
                    for hb in range(nH):
                        inst = e.matmul(
                            ps[pi][:, 0:w],
                            s_wst[wi][:, hb * 128:(hb + 1) * 128],
                            s_xnT[:, hb * TC:hb * TC + w],
                            start=(hb == 0), stop=(hb == nH - 1))
                    return inst

                kb.op("t", fn_g, reads=[b_wst[wi], b_xnT], writes=[b_ps[pi]])
                kb.op("s", lambda e, pi=pi, w=w: e.activation(
                    out=s_gt[pi][:, 0:w], in_=ps[pi][:, 0:w], func=AF.Silu),
                    reads=[b_ps[pi]], writes=[b_gt[pi]])
                kb.dma(gout[db, :, c0r:c0r + w], s_gt[pi][:, 0:w],
                       reads=[b_gt[pi]], ch="s")

            def load_ffn_slab(ob, wi):
                kb.dma(s_wst[wi][:, 0:nI * 128],
                       wT_scr[ob].rearrange("p ib o -> p (ib o)"),
                       reads=[b_wtscr[ob]], writes=[b_wst[wi]])

            load_ffn_slab(0, 0)
            load_ffn_slab(1, 1)
            for ob in range(nO):
                pi = ob % 2
                wi = ob % NW
                if ob + 2 < nO:
                    load_ffn_slab(ob + 2, (ob + 2) % NW)

                def fn_f(e, pi=pi, wi=wi, w=w):
                    for ib in range(nI):
                        inst = e.matmul(
                            ps[pi][:, 0:w],
                            s_wst[wi][:, ib * 128:(ib + 1) * 128],
                            s_xnT[:, ib * TC:ib * TC + w],
                            start=(ib == 0), stop=(ib == nI - 1))
                    return inst

                kb.op("t", fn_f, reads=[b_wst[wi], b_xnT], writes=[b_ps[pi]])
                kb.op("v", lambda e, pi=pi, ob=ob, w=w: e.tensor_scalar_add(
                    out=s_ac[pi][:, 0:w], in0=ps[pi][:, 0:w],
                    scalar1=s_fbias[:, ob:ob + 1]),
                    reads=[b_ps[pi], b_fbias], writes=[b_ac[pi]])
                kb.dma(aout[ob, :, c0r:c0r + w], s_ac[pi][:, 0:w],
                       reads=[b_ac[pi]], ch="s")

            for db in range(nD):
                bi = db % 2
                wi = db % NW
                kb.dma(s_ubf[bi][:, 0:w], u_bf_scr[db, :, c0r:c0r + w],
                       reads=[b_uscr[db]], writes=[b_ubf[bi]])
                kb.dma(s_wst[wi][:, 0:RC], wx_bf[db],
                       reads=[b_wxbf[db]], writes=[b_wst[wi]])
                for oi, (o0, ow) in enumerate(ogs):
                    kb.op("t", lambda e, oi=oi, o0=o0, ow=ow, bi=bi, wi=wi,
                          w=w, db=db:
                          e.matmul(ps[oi][:ow, 0:w],
                                   s_wst[wi][:, o0:o0 + ow],
                                   s_ubf[bi][:, 0:w],
                                   start=(db == 0), stop=(db == nD - 1)),
                          reads=[b_wst[wi], b_ubf[bi]], writes=[b_ps[oi]])
            for oi, (o0, ow) in enumerate(ogs):
                kb.op("s", lambda e, oi=oi, ow=ow, w=w: e.copy(
                    out=s_sp[:ow, oi * TC:oi * TC + w], in_=ps[oi][:ow, 0:w]),
                    reads=[b_ps[oi]], writes=[b_sp])

            for tt in range(ntt):
                di = tt % 2
                for oi, (o0, ow) in enumerate(ogs):
                    kb.op("t", lambda e, oi=oi, o0=o0, ow=ow, tt=tt: e.transpose(
                        psTf[:, o0:o0 + ow],
                        s_sp[:ow, oi * TC + tt * 128:oi * TC + tt * 128 + 128],
                        s_idf[:ow, 0:ow]),
                        reads=[b_sp, b_idf], writes=[b_psTf])
                kb.op("s", lambda e, di=di: e.copy(
                    out=s_spT[di][:, 0:RC], in_=psTf[:, 0:RC]),
                    reads=[b_psTf], writes=[b_spT[di]])
                for (s0, sl, j) in [(0, R, 0), (R, N, 1), (R + N, N, 2)]:
                    kb.op("s", lambda e, s0=s0, sl=sl, j=j, di=di: e.activation(
                        out=s_junk[:, 0:sl], in_=s_spT[di][:, s0:s0 + sl],
                        func=AF.Square, accum_out=s_rcol[:, j:j + 1]),
                        reads=[b_spT[di]], writes=[b_junk, b_rcol])
                    kb.op("s", lambda e, sl=sl, j=j: e.activation(
                        out=s_rcol[:, j:j + 1], in_=s_rcol[:, j:j + 1],
                        func=AF.Ln, scale=CST[1.0 / sl], bias=CST[EPS_RMS]),
                        reads=[b_rcol, b_cst], writes=[b_rcol])
                kb.op("s", lambda e: e.activation(
                    out=s_rcol[:, 0:3], in_=s_rcol[:, 0:3], func=AF.Exp,
                    scale=CST[-0.5]),
                    reads=[b_rcol, b_cst], writes=[b_rcol])
                kb.op("v", lambda e, di=di: e.tensor_scalar_mul(
                    out=s_rtmp[:, 0:R], in0=s_spT[di][:, 0:R],
                    scalar1=s_rcol[:, 0:1]),
                    reads=[b_spT[di], b_rcol], writes=[b_rtmp])
                kb.op("v", lambda e, di=di: e.tensor_mul(
                    out=s_dtn[di][:, 0:R], in0=s_rtmp[:, 0:R],
                    in1=s_lnw[:, 0:R]),
                    reads=[b_rtmp, b_lnw], writes=[b_dtn[di]])
                kb.op("v", lambda e, di=di: e.tensor_scalar_mul(
                    out=s_bcn[:, 0:N], in0=s_spT[di][:, R:R + N],
                    scalar1=s_rcol[:, 1:2]),
                    reads=[b_spT[di], b_rcol], writes=[b_bcn])
                kb.op("v", lambda e, di=di: e.tensor_scalar_mul(
                    out=s_bcn[:, N:2 * N], in0=s_spT[di][:, R + N:R + 2 * N],
                    scalar1=s_rcol[:, 2:3]),
                    reads=[b_spT[di], b_rcol], writes=[b_bcn])
                kb.op("v", lambda e: e.tensor_mul(
                    out=s_bcn[:, 0:2 * N], in0=s_bcn[:, 0:2 * N],
                    in1=s_lnw[:, R:R + 2 * N]),
                    reads=[b_bcn, b_lnw], writes=[b_bcn])
                kb.op("t", lambda e: e.transpose(
                    psTf[0:2 * N, 0:128], s_bcn[:, 0:2 * N], s_idf[:]),
                    reads=[b_bcn, b_idf], writes=[b_psTf])
                kb.op("s", lambda e: e.copy(
                    out=s_bcT[0:2 * N, 0:128], in_=psTf[0:2 * N, 0:128]),
                    reads=[b_psTf], writes=[b_bcT])
                kb.dma(bnT[:, c0r + tt * 128:c0r + tt * 128 + 128],
                       s_bcT[0:N, 0:128], reads=[b_bcT], ch="s")
                kb.dma(cnT[:, c0r + tt * 128:c0r + tt * 128 + 128],
                       s_bcT[N:2 * N, 0:128], reads=[b_bcT], ch="s")
                for ri, (r0, rw) in enumerate(rtiles):
                    kb.op("t", lambda e, r0=r0, rw=rw, ri=ri, di=di: e.transpose(
                        psTx[di][0:rw, ri * 128:ri * 128 + 128],
                        s_dtn[di][:, r0:r0 + rw], s_idb[:]),
                        reads=[b_dtn[di], b_idb], writes=[b_psTx[di]])
                for ri, (r0, rw) in enumerate(rtiles):
                    kb.op("s", lambda e, ri=ri, tt=tt, rw=rw, di=di: e.copy(
                        out=s_dtnT[:rw, ri * TC + tt * 128:ri * TC + tt * 128 + 128],
                        in_=psTx[di][0:rw, ri * 128:ri * 128 + 128]),
                        reads=[b_psTx[di]], writes=[b_dtnT])

            def load_wdt_slab(db, wi):
                for ri, (r0, rw) in enumerate(rtiles):
                    kb.dma(s_wst[wi][:rw, ri * 128:ri * 128 + 128],
                           wdt_bf[ri, 0:rw, db * 128:(db + 1) * 128],
                           reads=[b_wdtbf], writes=[b_wst[wi]])

            load_wdt_slab(0, 0)
            load_wdt_slab(1, 1)
            for db in range(nD):
                pi = db % 4
                wi = db % NW
                if db + 2 < nD:
                    load_wdt_slab(db + 2, (db + 2) % NW)

                def fn_d(e, pi=pi, wi=wi, w=w):
                    for ri, (r0, rw) in enumerate(rtiles):
                        inst = e.matmul(
                            ps[pi][:, 0:w],
                            s_wst[wi][:rw, ri * 128:ri * 128 + 128],
                            s_dtnT[:rw, ri * TC:ri * TC + w],
                            start=(ri == 0), stop=(ri == nR - 1))
                    return inst

                kb.op("t", fn_d, reads=[b_wst[wi], b_dtnT], writes=[b_ps[pi]])
                kb.op("s", lambda e, pi=pi, db=db, w=w: e.activation(
                    out=s_sf[0][:, 0:w], in_=ps[pi][:, 0:w], func=AF.Abs,
                    bias=s_dtbias[:, db:db + 1]),
                    reads=[b_ps[pi], b_dtbias], writes=[b_sf[0]])
                kb.op("s", lambda e, w=w: e.activation(
                    out=s_sf[1][:, 0:w], in_=s_sf[0][:, 0:w], func=AF.Exp,
                    scale=CST[-1.0]),
                    reads=[b_sf[0], b_cst], writes=[b_sf[1]])
                kb.op("v", lambda e, pi=pi, db=db, w=w: e.tensor_scalar(
                    out=s_sf[0][:, 0:w], in0=ps[pi][:, 0:w],
                    scalar1=s_dtbias[:, db:db + 1], scalar2=0.0,
                    op0=OP.add, op1=OP.max),
                    reads=[b_ps[pi], b_dtbias], writes=[b_sf[0]])
                kb.op("s", lambda e, w=w: e.activation(
                    out=s_sf[1][:, 0:w], in_=s_sf[1][:, 0:w], func=AF.Ln,
                    bias=1.0),
                    reads=[b_sf[1]], writes=[b_sf[1]])
                kb.op("v", lambda e, w=w, pi=pi: e.tensor_add(
                    out=s_dl[pi % 2][:, 0:w], in0=s_sf[0][:, 0:w],
                    in1=s_sf[1][:, 0:w]),
                    reads=[b_sf[0], b_sf[1]], writes=[b_dl[pi % 2]])
                kb.dma(dout[db, :, c0r:c0r + w], s_dl[pi % 2][:, 0:w],
                       reads=[b_dl[pi % 2]], ch="s")

        block = es.enter_context(nc.Block())
        kb.build_and_finish(block)
    return nc


def build_L2(c: Cfg):
    T, DS, N, SEG = c.T, c.DS, c.N, c.SEG
    nb = DS // 128
    nseg = T // SEG

    nc = bass.Bass("TRN2", target_bir_lowering=False, num_devices=8)
    dt = nc.dram_tensor
    delta = dt("delta", [nb, 128, T], F32, kind="ExternalInput").ap()
    u = dt("u", [nb, 128, T], F32, kind="ExternalInput").ap()
    gate = dt("gate", [nb, 128, T], F32, kind="ExternalInput").ap()
    bnT = dt("bnT", [N, T], F32, kind="ExternalInput").ap()
    cnT = dt("cnT", [N, T], F32, kind="ExternalInput").ap()
    a_sl = dt("a_sl", [128, nb * N], F32, kind="ExternalInput").ap()
    identf = dt("identf", [128, 128], F32, kind="ExternalInput").ap()
    dpar = dt("dpar", [128, nb], F32, kind="ExternalInput").ap()
    y = dt("y", [nb, 128, T], BF16, kind="ExternalOutput").ap()

    es = contextlib.ExitStack()
    with es:
        Sb = lambda nm, shp, dtp=F32: es.enter_context(nc.sbuf_tensor(nm, shp, dtp))
        kb = KB(nc, es, engines=("v", "s", "t", "y"))

        s_A = Sb("s_A", [128, nb * N]); b_A = Buf("A")
        s_D = Sb("s_D", [128, nb]); b_D = Buf("D")
        s_hl = Sb("s_hl", [128, nb * N])
        b_hl = [Buf(f"hl{i}") for i in range(nb * N)]
        s_Bb = [Sb(f"s_Bb_{i}", [128, SEG]) for i in range(N)]
        b_Bb = [Buf(f"Bb{i}") for i in range(N)]
        s_Cb = [Sb(f"s_Cb_{i}", [128, SEG]) for i in range(N)]
        b_Cb = [Buf(f"Cb{i}") for i in range(N)]
        s_d = [Sb(f"s_d_{i}", [128, SEG]) for i in range(2)]
        b_d = [Buf(f"d{i}") for i in range(2)]
        s_u = [Sb(f"s_u_{i}", [128, SEG]) for i in range(2)]
        b_u = [Buf(f"u{i}") for i in range(2)]
        s_du = [Sb(f"s_du_{i}", [128, SEG]) for i in range(2)]
        b_du = [Buf(f"du{i}") for i in range(2)]
        s_g = [Sb(f"s_g_{i}", [128, SEG]) for i in range(2)]
        b_g = [Buf(f"g{i}") for i in range(2)]
        s_ya = [Sb(f"s_ya_{i}", [128, SEG]) for i in range(2)]
        b_ya = [Buf(f"ya{i}") for i in range(2)]
        s_dA = [Sb(f"s_dA_{i}", [128, SEG]) for i in range(2)]
        b_dA = [Buf(f"dA{i}") for i in range(2)]
        s_dB = Sb("s_dB", [128, SEG]); b_dB = Buf("dB")
        s_h = Sb("s_h", [128, SEG]); b_h = Buf("h")
        s_hc = [Sb(f"s_hc_{i}", [128, SEG]) for i in range(2)]
        b_hc = [Buf(f"hc{i}") for i in range(2)]
        NHF = ceil_div(SEG, 512)
        WHF = min(512, SEG)
        psY = [[es.enter_context(nc.psum_tensor(f"psY_{i}_{hh}", [128, 512], F32))
                for hh in range(NHF)] for i in range(2)]
        b_psY = [[Buf(f"psY{i}_{hh}") for hh in range(NHF)] for i in range(2)]
        s_yb = [Sb(f"s_yb_{i}", [128, SEG], BF16) for i in range(2)]
        b_yb = [Buf(f"yb{i}") for i in range(2)]

        s_idf = Sb("s_idf", [128, 128], F32); b_idf = Buf("idf")
        kb.dma(s_idf[:], identf, writes=[b_idf])
        kb.dma(s_A[:], a_sl, writes=[b_A])
        kb.dma(s_D[:], dpar, writes=[b_D])

        for sg in range(nseg):
            sl = slice(sg * SEG, (sg + 1) * SEG)
            for n in range(N):
                kb.dma(s_Bb[n][:], bnT[n:n + 1, sl].partition_broadcast(128),
                       writes=[b_Bb[n]])
                kb.dma(s_Cb[n][:], cnT[n:n + 1, sl].partition_broadcast(128),
                       writes=[b_Cb[n]])
            for b in range(nb):
                i = b % 2
                kb.dma(s_d[i][:], delta[b, :, sl], writes=[b_d[i]])
                kb.dma(s_u[i][:], u[b, :, sl], writes=[b_u[i]])
                kb.dma(s_g[i][:], gate[b, :, sl], writes=[b_g[i]])
                kb.op("v", lambda e, i=i: e.tensor_mul(
                    out=s_du[i][:], in0=s_d[i][:], in1=s_u[i][:]),
                    reads=[b_d[i], b_u[i]], writes=[b_du[i]])
                for n in range(N):
                    hli = b * N + n
                    ai = n % 2
                    kb.op("s", lambda e, i=i, b=b, n=n, ai=ai: e.activation(
                        out=s_dA[ai][:], in_=s_d[i][:], func=AF.Exp,
                        scale=s_A[:, b * N + n:b * N + n + 1]),
                        reads=[b_d[i], b_A], writes=[b_dA[ai]])
                    kb.op("v", lambda e, i=i, n=n: e.tensor_mul(
                        out=s_dB[:], in0=s_du[i][:], in1=s_Bb[n][:]),
                        reads=[b_du[i], b_Bb[n]], writes=[b_dB])
                    if sg == 0:
                        kb.op("v", lambda e, ai=ai: e.tensor_tensor_scan(
                            out=s_h[:], data0=s_dA[ai][:], data1=s_dB[:],
                            initial=0.0, op0=OP.mult, op1=OP.add),
                            reads=[b_dA[ai], b_dB], writes=[b_h])
                    else:
                        kb.op("v", lambda e, ai=ai, hli=hli: e.tensor_tensor_scan(
                            out=s_h[:], data0=s_dA[ai][:], data1=s_dB[:],
                            initial=s_hl[:, hli:hli + 1],
                            op0=OP.mult, op1=OP.add),
                            reads=[b_dA[ai], b_dB, b_hl[hli]], writes=[b_h])
                    if sg < nseg - 1:
                        kb.op("s", lambda e, hli=hli: e.copy(
                            out=s_hl[:, hli:hli + 1], in_=s_h[:, SEG - 1:SEG]),
                            reads=[b_h], writes=[b_hl[hli]])
                    ci = n % 2
                    kb.op("v", lambda e, n=n, ci=ci: e.tensor_mul(
                        out=s_hc[ci][:], in0=s_h[:], in1=s_Cb[n][:]),
                        reads=[b_h, b_Cb[n]], writes=[b_hc[ci]])
                    for hh in range(NHF):
                        kb.op("t", lambda e, i=i, n=n, ci=ci, hh=hh: e.matmul(
                            psY[i][hh][:, 0:WHF],
                            s_idf[:],
                            s_hc[ci][:, hh * WHF:(hh + 1) * WHF],
                            start=(n == 0), stop=(n == N - 1)),
                            reads=[b_hc[ci], b_idf], writes=[b_psY[i][hh]])
                for hh in range(NHF):
                    kb.op("v", lambda e, i=i, b=b, hh=hh: e.scalar_tensor_tensor(
                        out=s_ya[i][:, hh * WHF:(hh + 1) * WHF],
                        in0=s_u[i][:, hh * WHF:(hh + 1) * WHF],
                        scalar=s_D[:, b:b + 1],
                        in1=psY[i][hh][:, 0:WHF], op0=OP.mult, op1=OP.add),
                        reads=[b_u[i], b_D, b_psY[i][hh]], writes=[b_ya[i]])
                kb.op("v", lambda e, i=i: e.tensor_mul(
                    out=s_yb[i][:], in0=s_ya[i][:], in1=s_g[i][:]),
                    reads=[b_ya[i], b_g[i]], writes=[b_yb[i]])
                kb.dma(y[b, :, sl], s_yb[i][:], reads=[b_yb[i]], ch="s")

        block = es.enter_context(nc.Block())
        kb.build_and_finish(block)
    return nc


def build_L3(c: Cfg):
    H, D = c.H, c.D
    S, TC = c.S, c.TC
    nH, nD = H // 128, D // 128
    chunks = [(i * TC, TC) for i in range(S // TC)]

    nc = bass.Bass("TRN2", target_bir_lowering=False, num_devices=8)
    dt = nc.dram_tensor
    w_out_t = dt("w_out_t", [nH, 128, nD, 128], F32, kind="ExternalInput").ap()
    yin = dt("yin", [nD, 128, S], BF16, kind="ExternalInput").ap()
    acc = dt("acc", [nH, 128, S], F32, kind="ExternalInput").ap()
    x = dt("x", [S, H], F32, kind="ExternalInput").ap()
    identf = dt("identf", [128, 128], F32, kind="ExternalInput").ap()
    out = dt("out", [S, H], F32, kind="ExternalOutput").ap()
    w_out_bf = dt("w_out_bf", [nH, 128, nD * 128], BF16, kind="Internal").ap()
    b_wobf = [Buf(f"wobf{i}") for i in range(nH)]

    es = contextlib.ExitStack()
    with es:
        Sb = lambda nm, shp, dtp=F32: es.enter_context(nc.sbuf_tensor(nm, shp, dtp))
        kb = KB(nc, es)

        s_idf = Sb("s_idf", [128, 128], F32); b_idf = Buf()
        kb.dma(s_idf[:], identf, writes=[b_idf])

        with contextlib.ExitStack() as es0:
            s_w32 = [es0.enter_context(
                nc.sbuf_tensor(f"s_w32_{i}", [128, nD * 128], F32))
                for i in range(2)]
            b_w32 = [Buf() for _ in range(2)]
            s_wcc = [es0.enter_context(
                nc.sbuf_tensor(f"s_wcc_{i}", [128, nD * 128], BF16))
                for i in range(2)]
            b_wcc = [Buf() for _ in range(2)]
            for hb in range(nH):
                i = hb % 2
                kb.dma(s_w32[i][:], w_out_t[hb].rearrange("p db oc -> p (db oc)"),
                       writes=[b_w32[i]])
                kb.op("s", lambda e, i=i: e.copy(out=s_wcc[i][:], in_=s_w32[i][:]),
                      reads=[b_w32[i]], writes=[b_wcc[i]])
                kb.dma(w_out_bf[hb], s_wcc[i][:], reads=[b_wcc[i]],
                       writes=[b_wobf[hb]], sem_buf=b_wcc[i], ch="s")

        kb.fence()
        s_yall = Sb("s_yall", [128, nD * TC], BF16); b_yall = Buf()
        s_wst = [Sb(f"s_wst_{i}", [128, nD * 128], BF16) for i in range(2)]
        b_wst = [Buf() for _ in range(2)]
        s_acc = Sb("s_acc", [128, TC]); b_acc = Buf()
        s_oh = Sb("s_oh", [128, TC]); b_oh = Buf()
        s_fin = [Sb(f"s_fin_{i}", [128, H]) for i in range(TC // 128)]
        b_fin = [Buf() for _ in range(TC // 128)]
        s_x = Sb("s_x", [128, H]); b_x = Buf()

        ps = [es.enter_context(nc.psum_tensor(f"ps_{i}", [128, 512], F32))
              for i in range(2)]
        b_ps = [Buf() for _ in range(2)]
        psT3 = [es.enter_context(nc.psum_tensor(f"psT3_{i}", [128, 128], F32))
                for i in range(2)]
        b_psT3 = [Buf(f"psT3{i}") for i in range(2)]

        for (c0, w) in chunks:
            for db in range(nD):
                kb.dma(s_yall[:, db * TC:db * TC + w], yin[db, :, c0:c0 + w],
                       writes=[b_yall])
            for hb in range(nH):
                pi = hb % 2
                kb.dma(s_wst[pi][:], w_out_bf[hb],
                       reads=[b_wobf[hb]], writes=[b_wst[pi]])

                def fn_o(e, pi=pi, w=w):
                    for db in range(nD):
                        inst = e.matmul(
                            ps[pi][:, 0:w],
                            s_wst[pi][:, db * 128:(db + 1) * 128],
                            s_yall[:, db * TC:db * TC + w],
                            start=(db == 0), stop=(db == nD - 1))
                    return inst

                kb.op("t", fn_o, reads=[b_wst[pi], b_yall], writes=[b_ps[pi]])
                kb.dma(s_acc[:, 0:w], acc[hb, :, c0:c0 + w], writes=[b_acc])
                kb.op("v", lambda e, pi=pi, w=w: e.tensor_add(
                    out=s_oh[:, 0:w], in0=ps[pi][:, 0:w], in1=s_acc[:, 0:w]),
                    reads=[b_ps[pi], b_acc], writes=[b_oh])
                for tt in range(w // 128):
                    zi = tt % 2
                    kb.op("t", lambda e, tt=tt, zi=zi: e.transpose(
                        psT3[zi][:], s_oh[:, tt * 128:(tt + 1) * 128], s_idf[:]),
                        reads=[b_oh, b_idf], writes=[b_psT3[zi]])
                    kb.op("s", lambda e, hb=hb, tt=tt, zi=zi: e.copy(
                        out=s_fin[tt][:, hb * 128:(hb + 1) * 128],
                        in_=psT3[zi][:]),
                        reads=[b_psT3[zi]], writes=[b_fin[tt]])
            for tt in range(w // 128):
                t0 = c0 + tt * 128
                kb.dma(s_x[:], x[t0:t0 + 128, :], writes=[b_x])
                kb.op("v", lambda e, tt=tt: e.tensor_add(
                    out=s_fin[tt][:], in0=s_fin[tt][:], in1=s_x[:]),
                    reads=[b_x], writes=[b_fin[tt]])
                kb.dma(out[t0:t0 + 128, :], s_fin[tt][:], reads=[b_fin[tt]],
                       ch="s")

        block = es.enter_context(nc.Block())
        kb.build_and_finish(block)
    return nc


def host_prepare(c: Cfg, inputs):
    H, D, R, N, K, G, GS = c.H, c.D, c.R, c.N, c.K, c.G, c.GS
    S, HB = c.S, c.HALO
    nH, nD, nO = H // 128, D // 128, H // 128

    xfull = np.asarray(inputs["L0_attn_hidden_states"], np.float32)
    B, T, _ = xfull.shape
    xf = xfull.reshape(B * T, H)
    W_in = np.asarray(inputs["L0_attn_W_in"], np.float32)
    w_in_t = np.ascontiguousarray(
        W_in.reshape(2 * D // 128, 128, nH, 128).transpose(0, 3, 2, 1))
    conv_w = np.ascontiguousarray(
        np.asarray(inputs["L0_attn_conv_w"], np.float32)
        .reshape(nD, 128, K).transpose(1, 0, 2).reshape(128, nD * K))
    conv_b = np.ascontiguousarray(
        np.asarray(inputs["L0_attn_conv_b"], np.float32).reshape(nD, 128).T)
    wxT = np.ascontiguousarray(np.asarray(inputs["L0_attn_Wx"], np.float32).T)
    ln_w_cat = np.concatenate([
        np.asarray(inputs["L0_attn_dt_ln_w"], np.float32),
        np.asarray(inputs["L0_attn_b_ln_w"], np.float32),
        np.asarray(inputs["L0_attn_c_ln_w"], np.float32)]).reshape(1, -1)
    wdtT = np.ascontiguousarray(np.asarray(inputs["L0_attn_Wdt"], np.float32).T)
    dt_bias = np.ascontiguousarray(
        np.asarray(inputs["L0_attn_dt_bias"], np.float32).reshape(nD, 128).T)
    qw = np.ascontiguousarray(
        np.asarray(inputs["L0_ffn_qweight"], np.int32).reshape(H, H))
    qscale = np.ascontiguousarray(
        np.asarray(inputs["L0_ffn_scales"], np.float32)
        .reshape(nO, 128, G).transpose(1, 0, 2).reshape(128, nO * G))
    qzero = np.ascontiguousarray(
        np.asarray(inputs["L0_ffn_scaled_zeros"], np.float32)
        .reshape(nO, 128, G).transpose(1, 0, 2).reshape(128, nO * G))
    fbias = np.ascontiguousarray(
        np.asarray(inputs["L0_ffn_bias"], np.float32).reshape(nO, 128).T)
    import ml_dtypes
    identb = np.eye(128, dtype=ml_dtypes.bfloat16)
    identf = np.eye(128, dtype=np.float32)

    in_maps1 = []
    for cc in range(NCORES):
        t0 = cc * S
        xe = np.zeros((HB + S, H), np.float32)
        xe[HB:] = xf[t0:t0 + S]
        bstart = (t0 // T) * T
        nh = min(HB, t0 - bstart)
        if nh > 0:
            xe[HB - nh:HB] = xf[t0 - nh:t0]
        in_maps1.append(dict(
            x=xe, w_in_t=w_in_t, conv_w=conv_w, conv_b=conv_b, wxT=wxT,
            ln_w_cat=ln_w_cat, wdtT=wdtT, dt_bias=dt_bias, qw=qw,
            qscale=qscale, qzero=qzero, fbias=fbias, identb=identb,
            identf=identf))
    return in_maps1


def host_l2_maps(c: Cfg, r1):
    H, D, N = c.H, c.D, c.N
    S = c.S
    nD = D // 128
    nbat = NCORES // c.B
    DS = c.DS
    nb = DS // 128

    def cat_t(key, nblk):
        outs = []
        for b in range(c.B):
            cores = [r1[b * nbat + i][key] for i in range(nbat)]
            outs.append(np.concatenate(cores, axis=2))
        return outs

    dlt = cat_t("dout", nD)
    uu = cat_t("uout", nD)
    gg = cat_t("gout", nD)
    bn = cat_t("bnT", None)
    cn = cat_t("cnT", None)
    A = np.asarray(None) if False else None
    in_maps2 = []
    for cc in range(NCORES):
        b = cc // (NCORES // c.B)
        q = cc % (NCORES // c.B)
        sl = slice(q * nb, (q + 1) * nb)
        in_maps2.append(dict(
            delta=np.ascontiguousarray(dlt[b][sl]),
            u=np.ascontiguousarray(uu[b][sl]),
            gate=np.ascontiguousarray(gg[b][sl]),
            bnT=np.ascontiguousarray(bn[b]),
            cnT=np.ascontiguousarray(cn[b]),
        ))
    return in_maps2


def run_all(c: Cfg, inputs, runner):
    import ml_dtypes
    H, D, N = c.H, c.D, c.N
    S = c.S
    nD, nH = D // 128, H // 128
    nbat = NCORES // c.B
    DS = c.DS
    nb = DS // 128

    in1 = host_prepare(c, inputs)
    nc1 = build_L1(c)
    r1 = runner(nc1, in1)

    A = np.asarray(inputs["L0_attn_A"], np.float32).reshape(nD, 128, N)
    Dp = np.asarray(inputs["L0_attn_D_param"], np.float32).reshape(nD, 128)
    in2 = []
    per_b = {}
    for b in range(c.B):
        cores = [r1[b * nbat + i] for i in range(nbat)]
        per_b[b] = dict(
            delta=np.concatenate([r["dout"] for r in cores], axis=2),
            u=np.concatenate([r["uout"] for r in cores], axis=2),
            gate=np.concatenate([r["gout"] for r in cores], axis=2),
            bnT=np.concatenate([r["bnT"] for r in cores], axis=1),
            cnT=np.concatenate([r["cnT"] for r in cores], axis=1),
        )
    for cc in range(NCORES):
        b, q = cc // nbat, cc % nbat
        sl = slice(q * nb, (q + 1) * nb)
        in2.append(dict(
            identf=np.eye(128, dtype=np.float32),
            delta=np.ascontiguousarray(per_b[b]["delta"][sl]),
            u=np.ascontiguousarray(per_b[b]["u"][sl]),
            gate=np.ascontiguousarray(per_b[b]["gate"][sl]),
            bnT=per_b[b]["bnT"], cnT=per_b[b]["cnT"],
            a_sl=np.ascontiguousarray(
                A[sl].transpose(1, 0, 2).reshape(128, nb * N)),
            dpar=np.ascontiguousarray(Dp[sl].T)))
    nc2 = build_L2(c)
    r2 = runner(nc2, in2)

    W_out = np.asarray(inputs["L0_attn_W_out"], np.float32)
    w_out_t = np.ascontiguousarray(
        W_out.reshape(nH, 128, nD, 128).transpose(0, 3, 2, 1))
    identf = np.eye(128, dtype=np.float32)
    y_b = {}
    for b in range(c.B):
        y_b[b] = np.concatenate(
            [r2[b * nbat + q]["y"] for q in range(nbat)], axis=0)
    xfull = np.asarray(inputs["L0_attn_hidden_states"], np.float32)
    xf = xfull.reshape(c.B * c.T, H)
    in3 = []
    for cc in range(NCORES):
        t0 = cc * S
        b = t0 // c.T
        tb = t0 - b * c.T
        in3.append(dict(
            w_out_t=w_out_t,
            yin=np.ascontiguousarray(y_b[b][:, :, tb:tb + S]),
            acc=r1[cc]["aout"],
            x=np.ascontiguousarray(xf[t0:t0 + S]),
            identf=identf))
    nc3 = build_L3(c)
    r3 = runner(nc3, in3)

    out = np.concatenate([r3[cc]["out"] for cc in range(NCORES)], axis=0)
    return out.reshape(c.B, c.T, H)


from concourse.bass_utils import run_bass_kernel_spmd


def _hw_runner(nc, in_maps):
    res = run_bass_kernel_spmd(nc, in_maps, core_ids=list(range(NCORES)))
    return res.results


def kernel(**inputs):
    c = Cfg()
    out = run_all(c, inputs, _hw_runner)
    return np.ascontiguousarray(out.astype(np.float32))
